# version 01
# baseline (speedup 1.0000x reference)
import math
import numpy as np
from contextlib import ExitStack

import concourse.bass as bass
import concourse.mybir as mybir
from concourse.bass_utils import run_bass_kernel_spmd
from concourse.bass import IndirectOffsetOnAxis

F32 = mybir.dt.float32
BF16 = mybir.dt.bfloat16
I32 = mybir.dt.int32
U32 = mybir.dt.uint32
AF = mybir.ActivationFunctionType
ALU = mybir.AluOpType
AX = mybir.AxisListType

S = 4096
D = 1024
NT = S // 128
NCH = S // 512
ENGINES = ("sync", "gpsimd", "tensor", "vector", "scalar")

import ml_dtypes
BF16_NP = ml_dtypes.bfloat16


class Stream:

    def __init__(self, P, name, nbuf=2):
        self.P = P
        self.nbuf = nbuf
        self.sems = [P.sem(f"{name}_{i}") for i in range(nbuf)]
        self.vals = {}

    def dma(self, eng, i, fns):
        s = self.sems[i % self.nbuf]
        if not isinstance(fns, (list, tuple)):
            fns = [fns]
        for fn in fns:
            self.P.op(eng, fn, s, 16)
        self.vals[i] = s.n
        return s.n

    def wait(self, eng, i):
        self.P.wait(eng, self.sems[i % self.nbuf], self.vals[i])


def bf16(a):
    return np.ascontiguousarray(np.asarray(a, np.float32).astype(BF16_NP))


def f32c(a):
    return np.ascontiguousarray(np.asarray(a, np.float32))


class Sem:
    __slots__ = ("h", "n")

    def __init__(self, h):
        self.h = h
        self.n = 0


class Prog:
    def __init__(self, name):
        self.name = name
        self.nc = bass.Bass("TRN2", target_bir_lowering=False)
        self.es = ExitStack()
        self._scopes = [self.es]
        self.sched = {e: [] for e in ENGINES}
        self.sems = {}
        self.host_inputs = {}
        self.core_inputs = {}
        self._final = False
        self._uid = 0
        self._ctile = self.es.enter_context(self.nc.sbuf_tensor("constsf32", [128, 48], F32))
        self._cmap = {}

    def uid(self):
        self._uid += 1
        return self._uid

    def din(self, name, arr=None, per_core=None):
        if per_core is not None:
            shape, dt = per_core
            return self.nc.dram_tensor(name, list(shape), dt, kind="ExternalInput").ap()
        a = np.ascontiguousarray(arr)
        dt = {np.dtype(np.float32): F32, np.dtype(np.int32): I32,
              np.dtype(np.uint32): U32, np.dtype(BF16_NP): BF16}[a.dtype]
        self.host_inputs[name] = a
        return self.nc.dram_tensor(name, list(a.shape), dt, kind="ExternalInput").ap()

    def dout(self, name, shape, dt=F32):
        return self.nc.dram_tensor(name, list(shape), dt, kind="ExternalOutput").ap()

    def dtmp(self, name, shape, dt):
        return self.nc.dram_tensor(name, list(shape), dt, kind="Internal").ap()

    def sbuf(self, name, shape, dt):
        return self._scopes[-1].enter_context(self.nc.sbuf_tensor(name, shape, dt))

    def psum(self, name, shape, dt=F32):
        return self._scopes[-1].enter_context(self.nc.psum_tensor(name, shape, dt))

    def push_scope(self):
        self._scopes.append(ExitStack())

    def pop_scope(self):
        self.barrier()
        self._scopes.pop().close()

    def sem(self, name):
        if name not in self.sems:
            self.sems[name] = Sem(self.es.enter_context(self.nc.semaphore(name)))
        return self.sems[name]

    def constcol(self, val):
        if val not in self._cmap:
            i = len(self._cmap)
            assert i < 48
            self._cmap[val] = i
            self.op("gpsimd", lambda e, i=i, v=float(val): e.memset(self._ctile[:, i:i + 1], v),
                    self.sem("gpc"), 1)
        return self._ctile[:, self._cmap[val]:self._cmap[val] + 1]

    def cwait(self, eng):
        if "gpc" in self.sems:
            self.wait(eng, self.sems["gpc"])

    def wait(self, eng, sem, val=None):
        v = sem.n if val is None else val
        if v <= 0:
            return
        self.sched[eng].append(lambda e, s=sem.h, v=v: e.wait_ge(s, v))

    def op(self, eng, fn, sem=None, k=1):
        if sem is None:
            self.sched[eng].append(fn)
            return None
        sem.n += k
        self.sched[eng].append(lambda e, f=fn, s=sem.h, kk=k: f(e).then_inc(s, kk))
        return sem.n

    def barrier(self, engines=ENGINES):
        vals = [(s.h, s.n) for s in self.sems.values() if s.n > 0]
        for eng in engines:
            for h, v in vals:
                self.sched[eng].append(lambda e, h=h, v=v: e.wait_ge(h, v))

    def finalize(self):
        if self._final:
            return
        self._final = True
        vals = [(s.h, s.n) for s in self.sems.values() if s.n > 0]
        for h, v in vals:
            self.sched["gpsimd"].append(lambda e, h=h, v=v: e.wait_ge(h, v))
        with self.nc.Block() as block:
            for eng in ENGINES:
                lst = self.sched[eng]
                if not lst:
                    continue
                def emit(e, lst=lst):
                    for fn in lst:
                        fn(e)
                getattr(block, eng)(emit)
        self.es.close()

    def run(self, core_arrays=None, n_cores=8, **kw):
        self.finalize()
        in_maps = []
        for c in range(n_cores):
            m = dict(self.host_inputs)
            for k, arrs in (core_arrays or {}).items():
                m[k] = np.ascontiguousarray(arrs[c])
            in_maps.append(m)
        return run_bass_kernel_spmd(self.nc, in_maps, core_ids=list(range(n_cores)), **kw)


def emit_ln(P, x_src, xln_dst, *, rms_w_bc=None, t_dst=None, rms_eps=None, xln32_dst=None,
            t32_dst=None):
    u = P.uid()
    dve, act = P.sem("dve"), P.sem("act")
    ldx = Stream(P, "lnld")
    stx = Stream(P, "lnst")
    xt = [P.sbuf(f"lnx{i}_{u}", [128, D], F32) for i in range(2)]
    xo = [P.sbuf(f"lno{i}_{u}", [128, D], BF16) for i in range(2)] if xln_dst is not None else None
    xo32 = [P.sbuf(f"lno32{i}_{u}", [128, D], F32) for i in range(2)] if xln32_dst is not None else None
    to = [P.sbuf(f"lnt{i}_{u}", [128, D], BF16) for i in range(2)] if t_dst is not None else None
    to32 = [P.sbuf(f"lnt32{i}_{u}", [128, D], F32) for i in range(2)] if t32_dst is not None else None
    trash = P.sbuf(f"lntrash_{u}", [128, D], F32)
    tmp = P.sbuf(f"lntmp_{u}", [128, D], F32)
    sc = P.sbuf(f"lnsc_{u}", [128, 12], F32)
    eps1c = P.constcol(1e-5)
    eps2c = P.constcol(rms_eps) if rms_eps is not None else None
    P.cwait("scalar")
    done_dve = [0, 0]
    done_out = [0, 0]
    last_sq = 0
    for i in range(NT):
        b = i % 2
        if done_dve[b]:
            P.wait("sync", dve, done_dve[b])
        ldx.dma("sync", i, lambda e, b=b, i=i: e.dma_start(out=xt[b][:], in_=x_src[i * 128:(i + 1) * 128, :]))
        ldx.wait("vector", i)
        P.op("vector", lambda e, b=b: e.reduce_sum(out=sc[:, 0:1], in_=xt[b][:], axis=AX.X))
        P.op("vector", lambda e, b=b: e.tensor_mul(out=trash[:], in0=xt[b][:], in1=xt[b][:]))
        P.op("vector", lambda e: e.reduce_sum(out=sc[:, 1:2], in_=trash[:], axis=AX.X), dve, 1)
        P.wait("vector", dve, dve.n)
        P.op("vector", lambda e: e.tensor_scalar_mul(out=sc[:, 2:3], in0=sc[:, 0:1], scalar1=1.0 / D), dve, 1)
        P.wait("vector", dve, dve.n)
        P.op("vector", lambda e: e.tensor_mul(out=sc[:, 3:4], in0=sc[:, 2:3], in1=sc[:, 2:3]), dve, 1)
        P.wait("vector", dve, dve.n)
        P.op("vector", lambda e: e.scalar_tensor_tensor(out=sc[:, 4:5], in0=sc[:, 1:2], scalar=1.0 / D,
                                                        in1=sc[:, 3:4], op0=ALU.mult, op1=ALU.subtract), dve, 1)
        P.wait("scalar", dve, dve.n)
        P.op("scalar", lambda e, bb=eps1c: e.activation(out=sc[:, 5:6], in_=sc[:, 4:5], func=AF.Sqrt, bias=bb),
             act, 1)
        P.wait("vector", act, act.n)
        P.op("vector", lambda e: e.reciprocal(out=sc[:, 6:7], in_=sc[:, 5:6]), dve, 1)
        P.wait("vector", dve, dve.n)
        if done_out[b]:
            stx.wait("vector", i - 2)
        last_marks = (rms_w_bc is None) and (xln32_dst is None)
        if xln_dst is not None:
            if last_marks and xln32_dst is None:
                P.op("vector", lambda e, b=b: e.tensor_scalar(out=xo[b][:], in0=xt[b][:], scalar1=sc[:, 2:3],
                                                              scalar2=sc[:, 6:7], op0=ALU.subtract, op1=ALU.mult),
                     dve, 1)
            else:
                P.op("vector", lambda e, b=b: e.tensor_scalar(out=xo[b][:], in0=xt[b][:], scalar1=sc[:, 2:3],
                                                              scalar2=sc[:, 6:7], op0=ALU.subtract, op1=ALU.mult))
        if xln32_dst is not None:
            if rms_w_bc is None:
                P.op("vector", lambda e, b=b: e.tensor_scalar(out=xo32[b][:], in0=xt[b][:], scalar1=sc[:, 2:3],
                                                              scalar2=sc[:, 6:7], op0=ALU.subtract, op1=ALU.mult),
                     dve, 1)
            else:
                P.op("vector", lambda e, b=b: e.tensor_scalar(out=xo32[b][:], in0=xt[b][:], scalar1=sc[:, 2:3],
                                                              scalar2=sc[:, 6:7], op0=ALU.subtract, op1=ALU.mult))
        if rms_w_bc is not None:
            P.op("vector", lambda e: e.tensor_mul(out=sc[:, 7:8], in0=sc[:, 6:7], in1=sc[:, 6:7]), dve, 1)
            P.wait("vector", dve, dve.n)
            P.op("vector", lambda e: e.tensor_mul(out=sc[:, 8:9], in0=sc[:, 4:5], in1=sc[:, 7:8]), dve, 1)
            P.wait("scalar", dve, dve.n)
            P.op("scalar", lambda e, bb=eps2c: e.activation(out=sc[:, 9:10], in_=sc[:, 8:9],
                                                            func=AF.Sqrt, bias=bb), act, 1)
            P.wait("vector", act, act.n)
            P.op("vector", lambda e: e.reciprocal(out=sc[:, 10:11], in_=sc[:, 9:10]), dve, 1)
            P.wait("vector", dve, dve.n)
            P.op("vector", lambda e: e.tensor_mul(out=sc[:, 11:12], in0=sc[:, 6:7], in1=sc[:, 10:11]), dve, 1)
            P.wait("vector", dve, dve.n)
            P.op("vector", lambda e, b=b: e.tensor_scalar(out=tmp[:], in0=xt[b][:], scalar1=sc[:, 2:3],
                                                          scalar2=sc[:, 11:12], op0=ALU.subtract, op1=ALU.mult))
            if t32_dst is not None:
                P.op("vector", lambda e, b=b: e.tensor_mul(out=to[b][:], in0=tmp[:], in1=rms_w_bc[:]))
                P.op("vector", lambda e, b=b: e.tensor_mul(out=to32[b][:], in0=tmp[:], in1=rms_w_bc[:]), dve, 1)
            else:
                P.op("vector", lambda e, b=b: e.tensor_mul(out=to[b][:], in0=tmp[:], in1=rms_w_bc[:]), dve, 1)
        done_dve[b] = dve.n
        P.wait("gpsimd", dve, dve.n)
        fns = []
        if xln_dst is not None:
            fns.append(lambda e, b=b, i=i: e.dma_start(out=xln_dst[i * 128:(i + 1) * 128, :], in_=xo[b][:]))
        if xln32_dst is not None:
            fns.append(lambda e, b=b, i=i: e.dma_start(out=xln32_dst[i * 128:(i + 1) * 128, :], in_=xo32[b][:]))
        if t_dst is not None:
            fns.append(lambda e, b=b, i=i: e.dma_start(out=t_dst[i * 128:(i + 1) * 128, :], in_=to[b][:]))
        if t32_dst is not None:
            fns.append(lambda e, b=b, i=i: e.dma_start(out=t32_dst[i * 128:(i + 1) * 128, :], in_=to32[b][:]))
        stx.dma("gpsimd", i, fns)
        done_out[b] = 1


def emit_linear_tm(P, lhsT_tiles, rhs_sb, fcols, evac, *, ps_list, n_d=8):
    mm = P.sem("mm")
    guards = {}
    nps = len(ps_list)
    ci = 0
    for i in range(NT):
        for c0 in range(0, fcols, 512):
            w = min(512, fcols - c0)
            ps = ps_list[ci % nps]
            if ci % nps in guards:
                gs, gv = guards[ci % nps]
                P.wait("tensor", gs, gv)
            for d in range(n_d):
                P.op("tensor", lambda e, i=i, c0=c0, w=w, d=d, ps=ps:
                     e.matmul(ps[0:128, 0:w], lhsT_tiles[d][:, i * 128:(i + 1) * 128],
                              rhs_sb[:, d * fcols + c0:d * fcols + c0 + w],
                              start=(d == 0), stop=(d == n_d - 1)), mm, 1)
            guards[ci % nps] = evac(i, c0, w, ps, mm.n)
            ci += 1


def load_cm(P, src_dram, prefix):
    u = P.uid()
    ldg = P.sem("ldg")
    tiles = [P.sbuf(f"{prefix}{d}_{u}", [128, S], BF16) for d in range(8)]
    for d in range(8):
        P.op("sync", lambda e, d=d: e.dma_start_transpose(out=tiles[d][:],
                                                          in_=src_dram[:, d * 128:(d + 1) * 128]), ldg, 16)
    P.wait("sync", ldg, ldg.n)
    return tiles, ldg.n


def load_cm_rows(P, src_dram, prefix):
    u = P.uid()
    ldg = P.sem("ldg")
    tiles = [P.sbuf(f"{prefix}{d}_{u}", [128, S], BF16) for d in range(8)]
    for d in range(8):
        P.op("sync", lambda e, d=d: e.dma_start(out=tiles[d][:],
                                                in_=src_dram[d * 128:(d + 1) * 128, :]), ldg, 16)
    P.wait("sync", ldg, ldg.n)
    return tiles, ldg.n


def load_w(P, w_dram, fout, prefix, n_d=8):
    u = P.uid()
    ldg = P.sem("ldg")
    wsb = P.sbuf(f"{prefix}_{u}", [128, n_d * fout], BF16)
    for d in range(n_d):
        P.op("sync", lambda e, d=d: e.dma_start(out=wsb[:, d * fout:(d + 1) * fout],
                                                in_=w_dram[d * 128:(d + 1) * 128, :]), ldg, 16)
    P.wait("sync", ldg, ldg.n)
    return wsb, ldg.n


def load_f32(P, src_dram, shape, prefix, dt=F32):
    u = P.uid()
    ldg = P.sem("ldg")
    t = P.sbuf(f"{prefix}_{u}", list(shape), dt)
    P.op("sync", lambda e: e.dma_start(out=t[:], in_=src_dram), ldg, 16)
    P.wait("sync", ldg, ldg.n)
    return t, ldg.n


def emit_linear(P, lhsT_tiles, wsb, fout, evac, ps_list, wait_before=(), n_d=8):
    mm, dve = P.sem("mm"), P.sem("dve")
    for eng, s_, v in wait_before:
        P.wait(eng, P.sems[s_] if isinstance(s_, str) else s_, v)
    ps_guard = {}
    ci = 0
    for i in range(NT):
        for c0 in range(0, fout, 512):
            w = min(512, fout - c0)
            pj = ci % len(ps_list)
            ps = ps_list[pj]
            if pj in ps_guard:
                P.wait("tensor", dve, ps_guard[pj])
            for d in range(n_d):
                P.op("tensor", lambda e, i=i, c0=c0, w=w, d=d, ps=ps:
                     e.matmul(ps[0:128, 0:w], lhsT_tiles[d][:, i * 128:(i + 1) * 128],
                              wsb[:, d * fout + c0:d * fout + c0 + w],
                              start=(d == 0), stop=(d == n_d - 1)), mm, 1)
            P.wait("vector", mm, mm.n)
            ps_guard[pj] = evac(i, c0, ps)
            ci += 1


def emit_attention(P, *, nheads, q_dram, k_dram, v_dram, ctxT_dram, causal, scale,
                   masks_bc=None, stage=99, qk_mode="pairs", qkv_rows=None):
    u = P.uid()
    mm, act, dve, gp = P.sem("mm"), P.sem("act"), P.sem("dve"), P.sem("gp")
    ldt = Stream(P, "ldt")
    ldv = P.sem("ldv")
    stc = Stream(P, "stc")
    HP = nheads * 65
    qa = [P.sbuf(f"qa{i}_{u}", [65, S], BF16) for i in range(2)]
    ka = [P.sbuf(f"ka{i}_{u}", [65, S], BF16) for i in range(2)]
    sq = P.sbuf(f"sq_{u}", [64, S], BF16)
    qnr = P.sbuf(f"qnr_{u}", [1, S], BF16)
    ktmp = P.sbuf(f"ktmp_{u}", [1, 512], F32)
    kmx = P.sbuf(f"kmx_{u}", [1, 20], F32)
    rowheads = qk_mode == "fusedrows"
    VHP = 65 if rowheads else HP
    vaug = P.sbuf(f"vaug_{u}", [128, NT * VHP], BF16)
    pbuf = P.sbuf(f"pbuf_{u}", [128, NT * 512], BF16)
    ctx_sb = [P.sbuf(f"ctxs{i}_{u}", [65, 512], F32) for i in range(2)]
    ctx_bf = [P.sbuf(f"ctxb{i}_{u}", [64, 512], BF16) for i in range(2)]
    rec = P.sbuf(f"rec_{u}", [1, 512], F32)
    ones1 = P.sbuf(f"ones1_{u}", [1, 128], F32)
    onesb = P.sbuf(f"onesb_{u}", [64, 1], BF16)
    scale_c = P.constcol(scale)
    P.cwait("scalar")
    ps_s = [P.psum(f"pss{i}_{u}", [128, 1024], F32) for i in range(2)]
    ps_c = [P.psum(f"psc{i}_{u}", [65, 512], F32) for i in range(2)]
    ps_n = P.psum(f"psn_{u}", [64, 512], F32)

    P.op("gpsimd", lambda e: e.memset(ones1[:], 1.0))
    P.op("gpsimd", lambda e: e.memset(onesb[:], 1.0))
    P.op("gpsimd", lambda e: e.memset(vaug[:], 1.0), gp, 1)
    P.wait("gpsimd", gp, gp.n)
    vstg = [P.sbuf(f"vstg{i}_{u}", [128, 64 if rowheads else nheads * 64], BF16) for i in range(2)]
    ldvs = Stream(P, "ldvs")
    vcopied = [0, 0]
    ldv_total = 0

    def load_vaug_colheads():
        nonlocal ldv_total
        for kt in range(NT):
            vb_ = kt % 2
            if vcopied[vb_]:
                P.wait("gpsimd", ldv, vcopied[vb_])
            ldvs.dma("gpsimd", kt, lambda e, kt=kt, vb_=vb_, w=nheads * 64:
                     e.dma_start(out=vstg[vb_][:], in_=v_dram[kt * 128:(kt + 1) * 128, 0:w]))
            ldvs.wait("vector", kt)
            P.wait("vector", gp, gp.n)
            dst = vaug[:, kt * HP:(kt + 1) * HP].rearrange("p (h c) -> p h c", c=65)[:, :, 0:64]
            P.op("vector", lambda e, kt=kt, vb_=vb_, dst=dst:
                 e.tensor_copy(out=dst, in_=vstg[vb_][:].rearrange("p (h c) -> p h c", c=64)), ldv, 1)
            vcopied[vb_] = ldv.n
        ldv_total = ldv.n

    def load_vaug_rowheads(h, mm_guard):
        nonlocal ldv_total
        for kt in range(NT):
            vb_ = kt % 2
            if vcopied[vb_]:
                P.wait("gpsimd", ldv, vcopied[vb_])
            if kt == 0 and mm_guard:
                P.wait("gpsimd", mm, mm_guard)
            ldvs.dma("gpsimd", h * NT + kt, lambda e, h=h, kt=kt, vb_=vb_:
                     e.dma_start(out=vstg[vb_][:], in_=qkv_rows[S * h + kt * 128:S * h + (kt + 1) * 128, 128:192]))
            ldvs.wait("vector", h * NT + kt)
            P.wait("vector", gp, gp.n)
            if kt == 0 and mm_guard:
                P.wait("vector", mm, mm_guard)
            dst = vaug[:, kt * 65:kt * 65 + 64]
            P.op("vector", lambda e, vb_=vb_, dst=dst: e.tensor_copy(out=dst, in_=vstg[vb_][:]), ldv, 1)
            vcopied[vb_] = ldv.n
        ldv_total = ldv.n

    if not rowheads:
        load_vaug_colheads()

    qst = [P.sbuf(f"qst{i}_{u}", [128, S], BF16) for i in range(2)]
    kst = [P.sbuf(f"kst{i}_{u}", [128, S], BF16) for i in range(2)] if not rowheads else None
    pair_copied = [0, 0]

    def pair_load(p):
        pb = p % 2
        if pair_copied[pb]:
            P.wait("sync", dve, pair_copied[pb])
        if rowheads:
            ldt.dma("sync", p, [
                lambda e, p=p, pb=pb: e.dma_start_transpose(out=qst[pb][:],
                                                            in_=qkv_rows[S * p:S * (p + 1), 0:128]),
            ])
        else:
            ldt.dma("sync", p, [
                lambda e, p=p, pb=pb: e.dma_start_transpose(out=qst[pb][:], in_=q_dram[:, 128 * p:128 * p + 128]),
                lambda e, p=p, pb=pb: e.dma_start_transpose(out=kst[pb][:], in_=k_dram[:, 128 * p:128 * p + 128]),
            ])

    if stage < 1:
        return
    pair_load(0)
    head_mm_end = [0] * (nheads + 1)
    head_act_end = [0] * (nheads + 1)
    last_exp = [0, 0]
    last_evac = [0, 0]
    last_ctx_mm = 0
    last_stc = {}
    for h in range(nheads):
        b = h % 2
        p = h if rowheads else h // 2
        pb = p % 2
        r0 = 0 if rowheads else 64 * (h % 2)
        ldt.wait("vector", p)
        if h >= 2:
            P.wait("vector", mm, head_mm_end[h - 2])
            P.wait("vector", act, head_act_end[h - 2])
        if rowheads:
            load_vaug_rowheads(h, head_mm_end[h - 1] if h >= 1 else 0)
            P.op("vector", lambda e, b=b, pb=pb: e.tensor_copy(out=qa[b][0:64, :], in_=qst[pb][0:64, :]))
            P.op("vector", lambda e, b=b, pb=pb: e.tensor_copy(out=ka[b][0:64, :], in_=qst[pb][64:128, :]),
                 dve, 1)
            pair_copied[pb] = dve.n
            if h + 1 < nheads:
                pair_load(h + 1)
        else:
            P.op("vector", lambda e, b=b, pb=pb, r0=r0: e.tensor_copy(out=qa[b][0:64, :],
                                                                      in_=qst[pb][r0:r0 + 64, :]))
            P.op("vector", lambda e, b=b, pb=pb, r0=r0: e.tensor_copy(out=ka[b][0:64, :],
                                                                      in_=kst[pb][r0:r0 + 64, :]), dve, 1)
            if h % 2 == 1:
                pair_copied[pb] = dve.n
                if h + 1 < nheads:
                    pair_load(p + 1)
        copy_done = dve.n
        if stage < 11:
            head_mm_end[h] = mm.n
            head_act_end[h] = act.n
            continue
        P.wait("scalar", dve, copy_done)
        if h >= 1:
            P.wait("scalar", mm, head_mm_end[h - 1])
            P.wait("tensor", dve, last_evac[0])
            P.wait("tensor", dve, last_evac[1])
        P.op("scalar", lambda e, b=b: e.activation(out=sq[:], in_=qa[b][0:64, :], func=AF.Square), act, 1)
        if stage >= 12:
            P.wait("tensor", act, act.n)
            for c in range(NCH):
                if c > 0:
                    P.wait("tensor", act, act.n)
                P.op("tensor", lambda e, c=c: e.matmul(ps_n[0:1, 0:512], onesb[:, 0:1],
                                                       sq[:, c * 512:(c + 1) * 512], start=True, stop=True), mm, 1)
                P.wait("scalar", mm, mm.n)
                P.op("scalar", lambda e, c=c: e.activation(out=qnr[0:1, c * 512:(c + 1) * 512],
                                                           in_=ps_n[0:1, 0:512], func=AF.Sqrt), act, 1)
        if stage >= 13:
            P.op("scalar", lambda e, b=b: e.activation(out=sq[:], in_=ka[b][0:64, :], func=AF.Square), act, 1)
            P.wait("tensor", act, act.n)
            for c in range(NCH):
                if c > 0:
                    P.wait("tensor", act, act.n)
                P.op("tensor", lambda e, c=c: e.matmul(ps_n[0:1, 0:512], onesb[:, 0:1],
                                                       sq[:, c * 512:(c + 1) * 512], start=True, stop=True), mm, 1)
                P.wait("scalar", mm, mm.n)
                P.op("scalar", lambda e, c=c: e.activation(out=ktmp[0:1, :], in_=ps_n[0:1, 0:512], func=AF.Sqrt),
                     act, 1)
                P.wait("vector", act, act.n)
                P.op("vector", lambda e, c=c: e.reduce_max(out=kmx[0:1, 2 + c:3 + c], in_=ktmp[0:1, :], axis=AX.X),
                     dve, 1)
                P.wait("scalar", dve, dve.n)
        if stage >= 14:
            P.op("vector", lambda e: e.reduce_max(out=kmx[0:1, 0:1], in_=kmx[0:1, 2:2 + NCH], axis=AX.X), dve, 1)
            P.wait("vector", dve, dve.n)
            P.op("vector", lambda e, b=b: e.tensor_scalar(out=qa[b][64:65, :], in0=qnr[0:1, :],
                                                          scalar1=kmx[0:1, 0:1], scalar2=-1.0,
                                                          op0=ALU.mult, op1=ALU.mult), dve, 1)
            P.wait("gpsimd", dve, dve.n)
            P.op("gpsimd", lambda e, b=b: e.memset(ka[b][64:65, :], 1.0), gp, 1)
        negm_dve, ones_gp = dve.n, gp.n
        if stage < 3:
            head_mm_end[h] = mm.n
            head_act_end[h] = act.n

        if stage < 15:
            continue
        first = True
        for qc in range(NCH):
            ktiles = list(range(min(NT, (qc + 1) * 4))) if causal else list(range(NT))
            for gi in range(0, len(ktiles), 2):
                grp = ktiles[gi:gi + 2]
                pg = (gi // 2) % 2
                if first:
                    P.wait("tensor", dve, negm_dve)
                    P.wait("tensor", gp, ones_gp)
                    P.wait("tensor", ldv, ldv_total)
                    first = False
                if last_exp[pg]:
                    P.wait("tensor", act, last_exp[pg])
                for j, kt in enumerate(grp):
                    P.op("tensor", lambda e, b=b, kt=kt, qc=qc, j=j, pg=pg:
                         e.matmul(ps_s[pg][0:128, j * 512:(j + 1) * 512],
                                  ka[b][:, kt * 128:(kt + 1) * 128],
                                  qa[b][:, qc * 512:(qc + 1) * 512], start=True, stop=True), mm, 1)
                P.wait("scalar", mm, mm.n)
                if last_ctx_mm:
                    P.wait("scalar", mm, last_ctx_mm)
                w = len(grp) * 512
                P.op("scalar", lambda e, pg=pg, kt0=grp[0], w=w, sc=scale_c:
                     e.activation(out=pbuf[:, kt0 * 512:kt0 * 512 + w], in_=ps_s[pg][0:128, 0:w],
                                  func=AF.Exp, scale=sc), act, 1)
                last_exp[pg] = act.n
            if causal:
                P.wait("vector", act, act.n)
                for j in range(4):
                    kt = qc * 4 + j
                    if kt < NT:
                        P.op("vector", lambda e, kt=kt, j=j:
                             e.tensor_mul(out=pbuf[:, kt * 512:(kt + 1) * 512],
                                          in0=pbuf[:, kt * 512:(kt + 1) * 512],
                                          in1=masks_bc[:, j * 512:(j + 1) * 512]), dve, 1)
            cb = qc % 2
            if stage < 16:
                continue
            P.wait("tensor", act, act.n)
            if causal:
                P.wait("tensor", dve, dve.n)
            if last_evac[cb]:
                P.wait("tensor", dve, last_evac[cb])
            hoff = 0 if rowheads else 65 * h
            for j, kt in enumerate(ktiles):
                P.op("tensor", lambda e, kt=kt, j=j, cb=cb, hoff=hoff, n=len(ktiles):
                     e.matmul(ps_c[cb][:, :], vaug[:, kt * VHP + hoff:kt * VHP + hoff + 65],
                              pbuf[:, kt * 512:(kt + 1) * 512], start=(j == 0), stop=(j == n - 1)), mm, 1)
            last_ctx_mm = mm.n
            P.wait("vector", mm, mm.n)
            P.op("vector", lambda e, cb=cb: e.tensor_copy(out=ctx_sb[cb][:, :], in_=ps_c[cb][:, :]), dve, 1)
            P.wait("vector", dve, dve.n)
            P.op("vector", lambda e, cb=cb: e.reciprocal(out=rec[:], in_=ctx_sb[cb][64:65, :]), dve, 1)
            P.wait("tensor", dve, dve.n)
            P.op("tensor", lambda e: e.matmul(ps_n[0:64, 0:512], ones1[0:1, 0:64], rec[:],
                                              start=True, stop=True), mm, 1)
            P.wait("vector", mm, mm.n)
            if (qc % 2) in last_stc:
                stc.wait("vector", last_stc[qc % 2])
            P.op("vector", lambda e, cb=cb: e.tensor_mul(out=ctx_bf[cb][:, :], in0=ctx_sb[cb][0:64, :],
                                                         in1=ps_n[0:64, 0:512]), dve, 1)
            last_evac[cb] = dve.n
            P.wait("gpsimd", dve, dve.n)
            stc.dma("gpsimd", h * NCH + qc, lambda e, cb=cb, h=h, qc=qc:
                    e.dma_start(out=ctxT_dram[64 * h:64 * h + 64, qc * 512:(qc + 1) * 512],
                                in_=ctx_bf[cb][:, :]))
            last_stc[qc % 2] = h * NCH + qc
        head_mm_end[h] = mm.n
        head_act_end[h] = act.n


def plan_rope_interleaved(P, src_f32, dst_bf, cos_bc, ssw_bc, t1, t2, mark=None):
    P.op("vector", lambda e: e.tensor_mul(out=t1[:], in0=src_f32[:], in1=cos_bc[:]))
    sw_src = src_f32.rearrange("p (j two) -> p j two", two=2)
    sw_dst = t2.rearrange("p (j two) -> p j two", two=2)
    ssw_v = ssw_bc.rearrange("p (j two) -> p j two", two=2)
    P.op("vector", lambda e: e.tensor_mul(out=sw_dst[:, :, 0:1], in0=sw_src[:, :, 1:2], in1=ssw_v[:, :, 0:1]))
    P.op("vector", lambda e: e.tensor_mul(out=sw_dst[:, :, 1:2], in0=sw_src[:, :, 0:1], in1=ssw_v[:, :, 1:2]))
    if mark is None:
        P.op("vector", lambda e: e.tensor_add(out=dst_bf[:], in0=t1[:], in1=t2[:]))
    else:
        P.op("vector", lambda e: e.tensor_add(out=dst_bf[:], in0=t1[:], in1=t2[:]), mark[0], mark[1])


def emit_proj_store(P, xln_dram, w_dram, fout, bias_bc_dram, dst_dram, rope_consts=None,
                    xlnT_tiles=None):
    u = P.uid()
    dve = P.sem("dve")
    xlnT = xlnT_tiles if xlnT_tiles is not None else load_cm(P, xln_dram, "xT")[0]
    wsb, n2 = load_w(P, w_dram, fout, "wp")
    bias_sb = load_f32(P, bias_bc_dram, (128, fout), "bp")[0] if bias_bc_dram is not None else None
    ps_list = [P.psum(f"psp{i}_{u}", [128, 512], F32) for i in range(4)]
    ob = [P.sbuf(f"pob{i}_{u}", [128, fout], BF16) for i in range(2)]
    sto = Stream(P, "psto")
    if rope_consts is not None:
        cos_d, ssw_d, rope_ranges = rope_consts
        cosb, n4 = load_f32(P, cos_d, (128, D), "cosb")
        sswb, n5 = load_f32(P, ssw_d, (128, D), "sswb")
        qtmp = P.sbuf(f"qtmp_{u}", [128, D], F32)
        t1 = P.sbuf(f"t1_{u}", [128, D], F32)
        t2 = P.sbuf(f"t2_{u}", [128, D], F32)
    else:
        rope_ranges = []
    ldg = P.sems["ldg"]
    stored = [False, False]

    def evac(i, c0, ps):
        b = i % 2
        if c0 == 0 and stored[b]:
            sto.wait("vector", i - 2)
        in_rope = any(lo <= c0 < hi for lo, hi in rope_ranges)
        if in_rope:
            lo, hi = next((l, h) for l, h in rope_ranges if l <= c0 < h)
            P.op("vector", lambda e, c0=c0, lo=lo, ps=ps:
                 e.tensor_add(out=qtmp[:, c0 - lo:c0 - lo + 512], in0=ps[0:128, 0:512],
                              in1=bias_sb[:, c0:c0 + 512]), dve, 1)
            if c0 + 512 >= hi:
                plan_rope_interleaved(P, qtmp, ob[b][:, lo:hi],
                                      cosb, sswb, t1, t2, mark=(dve, 1))
        else:
            if bias_sb is not None:
                P.op("vector", lambda e, b=b, c0=c0, ps=ps:
                     e.tensor_add(out=ob[b][:, c0:c0 + 512], in0=ps[0:128, 0:512],
                                  in1=bias_sb[:, c0:c0 + 512]), dve, 1)
            else:
                P.op("vector", lambda e, b=b, c0=c0, ps=ps:
                     e.tensor_copy(out=ob[b][:, c0:c0 + 512], in_=ps[0:128, 0:512]), dve, 1)
        if c0 + 512 >= fout:
            P.wait("gpsimd", dve, dve.n)
            sto.dma("gpsimd", i, lambda e, b=b, i=i:
                    e.dma_start(out=dst_dram[i * 128:(i + 1) * 128, :], in_=ob[b][:]))
            stored[b] = True
        return dve.n

    emit_linear(P, xlnT, wsb, fout, evac, ps_list,
                wait_before=[("tensor", "ldg", ldg.n), ("vector", "ldg", ldg.n)])


def emit_wo_residual(P, ctxT_dram, w_dram, bias_bc_dram, x_in, x_out, scale_col=None):
    u = P.uid()
    dve = P.sem("dve")
    ctxTs, _ = load_cm_rows(P, ctxT_dram, "cT")
    wsb, _ = load_w(P, w_dram, D, "wo")
    bias_sb = None
    if bias_bc_dram is not None:
        bias_sb, _ = load_f32(P, bias_bc_dram, (128, D), "bo")
    ps_list = [P.psum(f"psw{i}_{u}", [128, 512], F32) for i in range(4)]
    xres = [P.sbuf(f"xr{i}_{u}", [128, D], F32) for i in range(2)]
    xo = [P.sbuf(f"xo{i}_{u}", [128, D], F32) for i in range(2)]
    ldr = Stream(P, "ldr")
    sto = Stream(P, "sto")
    ldg = P.sems["ldg"]
    wb = [("tensor", "ldg", ldg.n), ("vector", "ldg", ldg.n)]
    loaded = [False, False]
    stored = [False, False]

    def pre(i):
        b = i % 2
        if loaded[b]:
            P.wait("sync", dve, pre.guard[b])
        ldr.dma("sync", i, lambda e, b=b, i=i: e.dma_start(out=xres[b][:], in_=x_in[i * 128:(i + 1) * 128, :]))
        loaded[b] = True
    pre.guard = [0, 0]

    def evac(i, c0, ps):
        b = i % 2
        if c0 == 0:
            ldr.wait("vector", i)
            if stored[b]:
                sto.wait("vector", i - 2)
        if bias_sb is not None:
            P.op("vector", lambda e, b=b, c0=c0, ps=ps:
                 e.tensor_add(out=xo[b][:, c0:c0 + 512], in0=ps[0:128, 0:512], in1=bias_sb[:, c0:c0 + 512]))
        else:
            P.op("vector", lambda e, b=b, c0=c0, ps=ps:
                 e.tensor_copy(out=xo[b][:, c0:c0 + 512], in_=ps[0:128, 0:512]))
        if scale_col is not None:
            P.op("vector", lambda e, b=b, c0=c0: e.tensor_scalar_mul(out=xo[b][:, c0:c0 + 512],
                                                                     in0=xo[b][:, c0:c0 + 512],
                                                                     scalar1=scale_col))
        P.op("vector", lambda e, b=b, c0=c0:
             e.tensor_add(out=xo[b][:, c0:c0 + 512], in0=xo[b][:, c0:c0 + 512],
                          in1=xres[b][:, c0:c0 + 512]), dve, 1)
        if c0 + 512 >= D:
            pre.guard[b] = dve.n
            P.wait("gpsimd", dve, dve.n)
            sto.dma("gpsimd", i, lambda e, b=b, i=i:
                    e.dma_start(out=x_out[i * 128:(i + 1) * 128, :], in_=xo[b][:]))
            stored[b] = True
        return dve.n

    mm = P.sem("mm")
    for eng, s_, v in wb:
        P.wait(eng, P.sems[s_], v)
    ps_guard = {}
    ci = 0
    for i in range(NT):
        pre(i)
        for c0 in range(0, D, 512):
            pj = ci % len(ps_list)
            ps = ps_list[pj]
            if pj in ps_guard:
                P.wait("tensor", dve, ps_guard[pj])
            for d in range(8):
                P.op("tensor", lambda e, i=i, c0=c0, d=d, ps=ps:
                     e.matmul(ps[0:128, 0:512], ctxTs[d][:, i * 128:(i + 1) * 128],
                              wsb[:, d * D + c0:d * D + c0 + 512],
                              start=(d == 0), stop=(d == 7)), mm, 1)
            P.wait("vector", mm, mm.n)
            ps_guard[pj] = evac(i, c0, ps)
            ci += 1


def plan_rope_half(P, src_f32, dst_bf, cos_t, sins_t, t1, t2, mark=None):
    P.op("vector", lambda e: e.tensor_mul(out=t1[:], in0=src_f32[:], in1=cos_t[:]))
    s3 = src_f32.rearrange("p (h v c) -> p h v c", v=2, c=32)
    d3 = t2.rearrange("p (h v c) -> p h v c", v=2, c=32)
    ss3 = sins_t.rearrange("p (h v c) -> p h v c", v=2, c=32)
    P.op("vector", lambda e: e.tensor_mul(out=d3[:, :, 0:1, :], in0=s3[:, :, 1:2, :], in1=ss3[:, :, 0:1, :]))
    P.op("vector", lambda e: e.tensor_mul(out=d3[:, :, 1:2, :], in0=s3[:, :, 0:1, :], in1=ss3[:, :, 1:2, :]))
    if mark is None:
        P.op("vector", lambda e: e.tensor_add(out=dst_bf[:], in0=t1[:], in1=t2[:]))
    else:
        P.op("vector", lambda e: e.tensor_add(out=dst_bf[:], in0=t1[:], in1=t2[:]), mark[0], mark[1])


def plan_rope_split(P, src_f32, dst_bf, cos_t, sin_t, th1, th2, mark=None):
    qe = src_f32.rearrange("p (h j v) -> p h j v", j=32, v=2)[:, :, :, 0:1]
    qo = src_f32.rearrange("p (h j v) -> p h j v", j=32, v=2)[:, :, :, 1:2]
    c3 = cos_t.rearrange("p (h j) -> p h j", j=32)
    s3 = sin_t.rearrange("p (h j) -> p h j", j=32)
    dfirst = dst_bf.rearrange("p (h v c) -> p h v c", v=2, c=32)[:, :, 0:1, :]
    dsecond = dst_bf.rearrange("p (h v c) -> p h v c", v=2, c=32)[:, :, 1:2, :]
    h1 = th1[:, 0:512].rearrange("p (h j) -> p h j", j=32)
    h2 = th2[:, 0:512].rearrange("p (h j) -> p h j", j=32)
    P.op("vector", lambda e: e.tensor_mul(out=h1[:], in0=qe[:, :, :, 0:1], in1=c3[:]))
    P.op("vector", lambda e: e.tensor_mul(out=h2[:], in0=qo[:, :, :, 0:1], in1=s3[:]))
    P.op("vector", lambda e: e.tensor_sub(out=dfirst[:], in0=h1[:], in1=h2[:]))
    P.op("vector", lambda e: e.tensor_mul(out=h1[:], in0=qe[:, :, :, 0:1], in1=s3[:]))
    P.op("vector", lambda e: e.tensor_mul(out=h2[:], in0=qo[:, :, :, 0:1], in1=c3[:]))
    if mark is None:
        P.op("vector", lambda e: e.tensor_add(out=dsecond[:], in0=h1[:], in1=h2[:]))
    else:
        P.op("vector", lambda e: e.tensor_add(out=dsecond[:], in0=h1[:], in1=h2[:]), mark[0], mark[1])


def emit_ctx_proj(P, lhsT_dram, w_dram, fout, pre, evac, n_ps=4):
    u = P.uid()
    mm, dve = P.sem("mm"), P.sem("dve")
    lhsTs, _ = load_cm_rows(P, lhsT_dram, "cp")
    wsb, _ = load_w(P, w_dram, fout, "wcp")
    ldg = P.sems["ldg"]
    ps_list = [P.psum(f"pscp{i}_{u}", [128, 512], F32) for i in range(n_ps)]
    P.wait("tensor", ldg, ldg.n)
    P.wait("vector", ldg, ldg.n)
    ps_guard = {}
    ci = 0
    for i in range(NT):
        if pre is not None:
            pre(i)
        for c0 in range(0, fout, 512):
            pj = ci % n_ps
            ps = ps_list[pj]
            if pj in ps_guard:
                P.wait("tensor", dve, ps_guard[pj])
            for d in range(8):
                P.op("tensor", lambda e, i=i, c0=c0, d=d, ps=ps:
                     e.matmul(ps[0:128, 0:512], lhsTs[d][:, i * 128:(i + 1) * 128],
                              wsb[:, d * fout + c0:d * fout + c0 + 512],
                              start=(d == 0), stop=(d == 7)), mm, 1)
            P.wait("vector", mm, mm.n)
            ps_guard[pj] = evac(i, c0, ps)
            ci += 1


def emit_qkv_rope_stream(P, srcT_tiles, w_dram, bias_bc_dram, dst_dram, tab_dram, mode):
    u = P.uid()
    dve = P.sem("dve")
    fout = 3 * D
    wsb, _ = load_w(P, w_dram, fout, "wqr")
    bias_sb = load_f32(P, bias_bc_dram, (128, fout), "bqr")[0] if bias_bc_dram is not None else None
    tabw = 2048 if mode == "half" else 1024
    tab = [P.sbuf(f"tab{i}_{u}", [128, tabw], BF16) for i in range(2)]
    ldt = Stream(P, "qr_ldt")
    sto = Stream(P, "qr_sto")
    qtmp = P.sbuf(f"qtmp_{u}", [128, D], F32)
    t1 = P.sbuf(f"t1_{u}", [128, D], F32)
    t2 = P.sbuf(f"t2_{u}", [128, D], F32)
    ob = [P.sbuf(f"qob{i}_{u}", [128, fout], BF16) for i in range(2)]
    ps_list = [P.psum(f"psqr{i}_{u}", [128, 512], F32) for i in range(4)]
    ldg = P.sems["ldg"]
    mm = P.sem("mm")
    P.wait("tensor", ldg, ldg.n)
    P.wait("vector", ldg, ldg.n)
    tab_free = [0, 0]
    stored = [False, False]
    ps_guard = {}
    ci = 0
    for i in range(NT):
        b = i % 2
        if tab_free[b]:
            P.wait("sync", dve, tab_free[b])
        ldt.dma("sync", i, lambda e, b=b, i=i: e.dma_start(out=tab[b][:], in_=tab_dram[i * 128:(i + 1) * 128, :]))
        for c in range(6):
            c0 = c * 512
            pj = ci % 4
            ps = ps_list[pj]
            if pj in ps_guard:
                P.wait("tensor", dve, ps_guard[pj])
            for d in range(8):
                P.op("tensor", lambda e, i=i, c0=c0, d=d, ps=ps:
                     e.matmul(ps[0:128, 0:512], srcT_tiles[d][:, i * 128:(i + 1) * 128],
                              wsb[:, d * fout + c0:d * fout + c0 + 512],
                              start=(d == 0), stop=(d == 7)), mm, 1)
            P.wait("vector", mm, mm.n)
            if c == 0 and stored[b]:
                sto.wait("vector", i - 2)
            if c < 4:
                lo = (c // 2) * 1024
                off = c0 - lo
                if bias_sb is not None:
                    P.op("vector", lambda e, off=off, c0=c0, ps=ps:
                         e.tensor_add(out=qtmp[:, off:off + 512], in0=ps[0:128, 0:512],
                                      in1=bias_sb[:, c0:c0 + 512]), dve, 1)
                else:
                    P.op("vector", lambda e, off=off, ps=ps:
                         e.tensor_copy(out=qtmp[:, off:off + 512], in_=ps[0:128, 0:512]), dve, 1)
                if c % 2 == 1:
                    ldt.wait("vector", i)
                    if mode == "half":
                        plan_rope_half(P, qtmp, ob[b][:, lo:lo + 1024], tab[b][:, 0:1024],
                                       tab[b][:, 1024:2048], t1, t2, mark=(dve, 1))
                    else:
                        plan_rope_split(P, qtmp, ob[b][:, lo:lo + 1024], tab[b][:, 0:512],
                                        tab[b][:, 512:1024], t1, t2, mark=(dve, 1))
            else:
                if bias_sb is not None:
                    P.op("vector", lambda e, b=b, c0=c0, ps=ps:
                         e.tensor_add(out=ob[b][:, c0:c0 + 512], in0=ps[0:128, 0:512],
                                      in1=bias_sb[:, c0:c0 + 512]), dve, 1)
                else:
                    P.op("vector", lambda e, b=b, c0=c0, ps=ps:
                         e.tensor_copy(out=ob[b][:, c0:c0 + 512], in_=ps[0:128, 0:512]), dve, 1)
            ps_guard[pj] = dve.n
            ci += 1
        tab_free[b] = dve.n
        P.wait("gpsimd", dve, dve.n)
        sto.dma("gpsimd", i, lambda e, b=b, i=i: e.dma_start(out=dst_dram[i * 128:(i + 1) * 128, :],
                                                             in_=ob[b][:]))
        stored[b] = True


def plan_rms_stats(P, hb, trash, sc, c0, eps_col, out_col):
    dve, act = P.sem("dve"), P.sem("act")
    P.op("vector", lambda e: e.tensor_mul(out=trash[:], in0=hb[:], in1=hb[:]))
    P.op("vector", lambda e: e.reduce_sum(out=sc[:, c0:c0 + 1], in_=trash[:], axis=AX.X), dve, 1)
    P.wait("vector", dve, dve.n)
    P.op("vector", lambda e: e.tensor_scalar_mul(out=sc[:, c0 + 1:c0 + 2], in0=sc[:, c0:c0 + 1],
                                                 scalar1=1.0 / D), dve, 1)
    P.wait("scalar", dve, dve.n)
    P.op("scalar", lambda e: e.activation(out=sc[:, c0 + 2:c0 + 3], in_=sc[:, c0 + 1:c0 + 2],
                                          func=AF.Sqrt, bias=eps_col), act, 1)
    P.wait("vector", act, act.n)
    P.op("vector", lambda e: e.reciprocal(out=out_col, in_=sc[:, c0 + 2:c0 + 3]), dve, 1)
    P.wait("vector", dve, dve.n)


def emit_proj_cm(P, xlnT, wT_dram, dst_cm_dram, name="", gelu_bias_cm=None):
    u = P.uid()
    mm, dve, act = P.sem("mm"), P.sem("dve"), P.sem("act")
    wsb, _ = load_w(P, wT_dram, D, "wcm")
    ldg = P.sems["ldg"]
    ps_list = [P.psum(f"psm{i}_{u}", [128, 512], F32) for i in range(2)]
    ob = [P.sbuf(f"cob{i}_{u}", [128, 512], BF16) for i in range(2)]
    sto = Stream(P, f"csto{name}_{u}")
    P.wait("tensor", ldg, ldg.n)
    ps_guard = {}
    stored = {}
    ci = 0
    for ft in range(8):
        for qc in range(NCH):
            pj = ci % 2
            if pj in ps_guard:
                P.wait("tensor", ps_guard[pj][0], ps_guard[pj][1])
            for d in range(8):
                P.op("tensor", lambda e, ft=ft, qc=qc, d=d, pj=pj:
                     e.matmul(ps_list[pj][:, :], wsb[:, d * D + ft * 128:d * D + (ft + 1) * 128],
                              xlnT[d][:, qc * 512:(qc + 1) * 512], start=(d == 0), stop=(d == 7)), mm, 1)
            if gelu_bias_cm is not None:
                P.wait("scalar", mm, mm.n)
                if pj in stored:
                    sto.wait("scalar", stored[pj])
                P.op("scalar", lambda e, ft=ft, pj=pj: e.activation(out=ob[pj][:], in_=ps_list[pj][:, :],
                     func=AF.Gelu, bias=gelu_bias_cm[:, ft:ft + 1]), act, 1)
                ps_guard[pj] = (act, act.n)
                P.wait("gpsimd", act, act.n)
            else:
                P.wait("vector", mm, mm.n)
                if pj in stored:
                    sto.wait("vector", stored[pj])
                P.op("vector", lambda e, pj=pj: e.tensor_copy(out=ob[pj][:], in_=ps_list[pj][:, :]), dve, 1)
                ps_guard[pj] = (dve, dve.n)
                P.wait("gpsimd", dve, dve.n)
            sto.dma("gpsimd", ci, lambda e, ft=ft, qc=qc, pj=pj:
                    e.dma_start(out=dst_cm_dram[ft * 128:(ft + 1) * 128, qc * 512:(qc + 1) * 512],
                                in_=ob[pj][:]))
            stored[pj] = ci
            ci += 1


def build_l1_attn_a(W):
    P = Prog("l1aa")
    x_in = P.din("x", per_core=((S, D), F32))
    x_out = P.dout("x_out", [S, D], F32)
    wqT = P.din("wqT", bf16(np.asarray(W["Wq"], np.float32).T))
    wkT = P.din("wkT", bf16(np.asarray(W["Wk"], np.float32).T))
    wvT = P.din("wvT", bf16(np.asarray(W["Wv"], np.float32).T))
    woT = P.din("woT", bf16(np.asarray(W["Wo"], np.float32).T))
    dl_d = P.din("dlog", f32c(np.asarray(W["decay_logit"], np.float32).reshape(1, 1)))
    osc_d = P.din("oscale", f32c(np.asarray(W["out_scale"], np.float32).reshape(1, 1)))
    ks = np.arange(128, dtype=np.float32)[:, None]
    jt = np.arange(512, dtype=np.float32)[None, :]
    dmat_d = P.din("dmat", f32c(ks - jt))
    masks = np.stack([((ks - jt + 128.0 * v) > 0).astype(np.float32) for v in range(4)])
    masks_d = P.din("dmasks", f32c(masks.transpose(1, 0, 2).reshape(128, 4 * 512)))
    cexp_d = P.din("cexp", f32c((128.0 * np.arange(32, dtype=np.float32) - 1.0).reshape(1, 32)))

    xln = P.dtmp("xln", [S, D], BF16)
    qT_s = P.dtmp("qT_s", [D, S], BF16)
    kT_s = P.dtmp("kT_s", [D, S], BF16)
    v_s = P.dtmp("v_s", [S, D], BF16)
    ctxT = P.dtmp("ctxT", [D, S], BF16)

    P.push_scope()
    emit_ln(P, x_in, xln)
    P.pop_scope()

    P.push_scope()
    xlnT, _ = load_cm(P, xln, "xT")
    P.wait("vector", P.sems["ldg"], P.sems["ldg"].n)
    emit_proj_cm(P, xlnT, wqT, qT_s, name="q")
    emit_proj_cm(P, xlnT, wkT, kT_s, name="k")
    emit_proj_store(P, None, wvT, D, None, v_s, xlnT_tiles=xlnT)
    P.pop_scope()

    P.push_scope()
    u = P.uid()
    mm, dve, act, gp = P.sem("mm"), P.sem("dve"), P.sem("act"), P.sem("gp")
    ldg = P.sem("ldg")
    kcm = [P.sbuf(f"kcm{d}_{u}", [128, S], BF16) for d in range(8)]
    for d in range(8):
        P.op("sync", lambda e, d=d: e.dma_start(out=kcm[d][:], in_=kT_s[d * 128:(d + 1) * 128, :]), ldg, 16)
    dmat_sb, _ = load_f32(P, dmat_d, (128, 512), "dm")
    masks_sb, _ = load_f32(P, masks_d, (128, 4 * 512), "mk")
    cexp_sb, _ = load_f32(P, cexp_d, (1, 32), "ce")
    dl_sb, _ = load_f32(P, dl_d, (1, 1), "dl")
    sc1 = P.sbuf(f"sc1_{u}", [1, 4], F32)
    lnd_bc = P.sbuf(f"lndb_{u}", [128, 1], F32)
    crow = P.sbuf(f"crow_{u}", [1, 32], F32)
    c_bc = P.sbuf(f"cbc_{u}", [128, 32], F32)
    ones1 = P.sbuf(f"o1_{u}", [1, 128], F32)
    emt = P.sbuf(f"em_{u}", [128, 5 * 512], F32)
    pbuf = P.sbuf(f"pbuf_{u}", [128, NT * 512], BF16)
    qch = [P.sbuf(f"qch{i}_{u}", [128, 8 * 512], BF16) for i in range(2)]
    vtl = [P.sbuf(f"vtl{i}_{u}", [128, D], BF16) for i in range(2)]
    ob = [P.sbuf(f"aob{i}_{u}", [128, 512], BF16) for i in range(2)]
    ps_s = [P.psum(f"pss{i}_{u}", [128, 512], F32) for i in range(2)]
    ps_c = [P.psum(f"psc{i}_{u}", [128, 512], F32) for i in range(4)]
    lnc32 = P.constcol(-math.log(32.0))
    P.cwait("scalar")
    P.op("gpsimd", lambda e: e.memset(ones1[:], 1.0), gp, 1)
    ldg_all = ldg.n
    P.wait("scalar", ldg, ldg_all)
    P.op("scalar", lambda e: e.activation(out=sc1[0:1, 0:1], in_=dl_sb[0:1, 0:1], func=AF.Sigmoid), act, 1)
    P.wait("scalar", act, act.n)
    P.op("scalar", lambda e: e.activation(out=sc1[0:1, 1:2], in_=sc1[0:1, 0:1], func=AF.Ln), act, 1)
    P.wait("tensor", act, act.n)
    P.wait("tensor", gp, gp.n)
    P.op("tensor", lambda e: e.matmul(ps_s[0][0:128, 0:1], ones1[0:1, 0:128], sc1[0:1, 1:2],
                                      start=True, stop=True), mm, 1)
    P.wait("vector", mm, mm.n)
    P.op("vector", lambda e: e.tensor_copy(out=lnd_bc[:], in_=ps_s[0][0:128, 0:1]), dve, 1)
    P.wait("scalar", dve, dve.n)
    P.op("scalar", lambda e: e.activation(out=emt[:, 4 * 512:5 * 512], in_=dmat_sb[:], func=AF.Exp,
                                          scale=lnd_bc[:, 0:1]), act, 1)
    P.wait("vector", act, act.n)
    for v in range(4):
        P.op("vector", lambda e, v=v: e.tensor_mul(out=emt[:, v * 512:(v + 1) * 512],
                                                   in0=emt[:, 4 * 512:5 * 512],
                                                   in1=masks_sb[:, v * 512:(v + 1) * 512]), dve, 1)
    P.op("scalar", lambda e: e.activation(out=crow[:], in_=cexp_sb[:], func=AF.Exp,
                                          scale=sc1[0:1, 1:2], bias=lnc32[0:1, :]), act, 1)
    P.wait("tensor", act, act.n)
    P.op("tensor", lambda e: e.matmul(ps_s[0][0:128, 0:32], ones1[0:1, 0:128], crow[:],
                                      start=True, stop=True), mm, 1)
    P.wait("vector", mm, mm.n)
    P.op("vector", lambda e: e.tensor_copy(out=c_bc[:], in_=ps_s[0][0:128, 0:32]), dve, 1)
    setup_dve = dve.n
    P.wait("tensor", dve, setup_dve)

    ldq = Stream(P, "ldq")
    ldvt = Stream(P, "ldvt")
    sto = Stream(P, "sto")

    def load_qchunk(qc):
        b = qc % 2
        ldq.dma("sync", qc, [
            (lambda e, d=d, qc=qc, b=b: e.dma_start(out=qch[b][:, d * 512:(d + 1) * 512],
                                                    in_=qT_s[d * 128:(d + 1) * 128, qc * 512:(qc + 1) * 512]))
            for d in range(8)])

    load_qchunk(0)
    last_p = [0, 0]
    last_ctx = 0
    qch_free = [0, 0]
    vt_used = [0, 0]
    ev_guard = {}
    st_guard = {}
    vslot = 0
    for qc in range(NCH):
        b = qc % 2
        ktiles = list(range(4 * qc, NT))
        ldq.wait("tensor", qc)
        if qc + 1 < NCH:
            if qch_free[(qc + 1) % 2]:
                P.wait("sync", mm, qch_free[(qc + 1) % 2])
            load_qchunk(qc + 1)
        for j, kt in enumerate(ktiles):
            pj = j % 2
            if last_p[pj]:
                P.wait("tensor", dve, last_p[pj])
            for d in range(8):
                P.op("tensor", lambda e, kt=kt, qc=qc, d=d, pj=pj, b=b:
                     e.matmul(ps_s[pj][:, :], kcm[d][:, kt * 128:(kt + 1) * 128],
                              qch[b][:, d * 512:(d + 1) * 512], start=(d == 0), stop=(d == 7)), mm, 1)
            P.wait("vector", mm, mm.n)
            if last_ctx:
                P.wait("vector", mm, last_ctx)
            v_ = min(j, 4)
            P.op("vector", lambda e, kt=kt, pj=pj, j=j, v_=v_:
                 e.scalar_tensor_tensor(out=pbuf[:, kt * 512:(kt + 1) * 512], in0=ps_s[pj][:, :],
                                        scalar=c_bc[:, j:j + 1], in1=emt[:, v_ * 512:(v_ + 1) * 512],
                                        op0=ALU.mult, op1=ALU.mult), dve, 1)
            last_p[pj] = dve.n
        p_done = dve.n
        P.wait("tensor", dve, p_done)
        for half in range(2):
            for j, kt in enumerate(ktiles):
                vb = vslot % 2
                if vt_used[vb]:
                    P.wait("sync", mm, vt_used[vb])
                ldvt.dma("sync", vslot, lambda e, kt=kt, vb=vb:
                         e.dma_start(out=vtl[vb][:], in_=v_s[kt * 128:(kt + 1) * 128, :]))
                ldvt.wait("tensor", vslot)
                for dv in range(4):
                    dva = half * 4 + dv
                    P.op("tensor", lambda e, kt=kt, vb=vb, dv=dv, dva=dva, j=j, n=len(ktiles):
                         e.matmul(ps_c[dv][:, :], vtl[vb][:, dva * 128:(dva + 1) * 128],
                                  pbuf[:, kt * 512:(kt + 1) * 512], start=(j == 0), stop=(j == n - 1)),
                         mm, 1)
                vt_used[vb] = mm.n
                vslot += 1
            for dv in range(4):
                dva = half * 4 + dv
                eb = dva % 2
                P.wait("vector", mm, mm.n)
                if (eb in st_guard):
                    sto.wait("vector", st_guard[eb])
                P.op("vector", lambda e, dv=dv, eb=eb: e.tensor_copy(out=ob[eb][:], in_=ps_c[dv][:, :]), dve, 1)
                ev_guard[dv] = dve.n
                P.wait("gpsimd", dve, dve.n)
                sto.dma("gpsimd", qc * 8 + dva, lambda e, dva=dva, qc=qc, eb=eb:
                        e.dma_start(out=ctxT[dva * 128:(dva + 1) * 128, qc * 512:(qc + 1) * 512],
                                    in_=ob[eb][:]))
                st_guard[eb] = qc * 8 + dva
            if half == 0:
                P.wait("tensor", dve, dve.n)
        last_ctx = mm.n
        qch_free[b] = mm.n
    P.pop_scope()

    P.push_scope()
    osc_sb, _ = load_f32(P, osc_d, (1, 1), "osc")
    oscb = P.sbuf(f"oscb_{P.uid()}", [128, 1], F32)
    ones1b = P.sbuf(f"o1b_{P.uid()}", [1, 128], F32)
    ps_t = P.psum(f"pst_{P.uid()}", [128, 1], F32)
    gp = P.sems["gp"]
    P.op("gpsimd", lambda e: e.memset(ones1b[:], 1.0), gp, 1)
    P.wait("tensor", gp, gp.n)
    P.wait("tensor", P.sems["ldg"], P.sems["ldg"].n)
    P.op("tensor", lambda e: e.matmul(ps_t[0:128, 0:1], ones1b[0:1, 0:128], osc_sb[0:1, 0:1],
                                      start=True, stop=True), P.sems["mm"], 1)
    P.wait("vector", P.sems["mm"], P.sems["mm"].n)
    P.op("vector", lambda e: e.tensor_copy(out=oscb[:], in_=ps_t[0:128, 0:1]), P.sems["dve"], 1)
    P.wait("vector", P.sems["dve"], P.sems["dve"].n)
    emit_wo_residual(P, ctxT, woT, None, x_in, x_out, scale_col=oscb[:, 0:1])
    P.pop_scope()
    return P


def build_l1_attn_b(W):
    P = Prog("l1ab")
    x_in = P.din("x", per_core=((S, D), F32))
    x_out = P.dout("x_out", [S, D], F32)
    wqkvT = P.din("wqkvT", bf16(np.asarray(W["W_qkv"], np.float32).T))
    woT = P.din("woT", bf16(np.asarray(W["W_out"], np.float32).T))
    bqkv_d = P.din("bqkv_bc", f32c(np.broadcast_to(np.asarray(W["b_qkv"], np.float32), (128, 3072))))
    bout_d = P.din("bout_bc", f32c(np.broadcast_to(np.asarray(W["b_out"], np.float32), (128, D))))

    xln = P.dtmp("xln", [S, D], BF16)
    qkv_s = P.dtmp("qkv_s", [S, 3 * D], BF16)
    ctxT = P.dtmp("ctxT", [D, S], BF16)

    P.push_scope()
    emit_ln(P, x_in, xln)
    P.pop_scope()

    P.push_scope()
    emit_proj_store(P, xln, wqkvT, 3 * D, bqkv_d, qkv_s)
    P.pop_scope()

    P.push_scope()
    qkv_rows = qkv_s.rearrange("s (j c) -> (s j) c", c=192)
    emit_attention(P, nheads=16, q_dram=None, k_dram=None, v_dram=None, ctxT_dram=ctxT,
                   causal=False, scale=1.0 / 8.0, qk_mode="fusedrows", qkv_rows=qkv_rows)
    P.pop_scope()

    P.push_scope()
    emit_wo_residual(P, ctxT, woT, bout_d, x_in, x_out)
    P.pop_scope()
    return P


def build_l0_attn_b(W, stop_after=None, nheads=16, taps=False):
    P = Prog("l0ab")
    x_in = P.din("x", per_core=((S, D), F32))
    x_out = P.dout("x_out", [S, D], F32)
    wqkv = P.din("wqkv", bf16(W["w_qkv"]))
    wfc = P.din("wfc", bf16(W["w_fc"]))
    bqkv_d = P.din("bqkv_bc", f32c(np.broadcast_to(W["b_qkv"].astype(np.float32), (128, 3072))))
    bfc_d = P.din("bfc_bc", f32c(np.broadcast_to(W["b_fc"].astype(np.float32), (128, D))))
    Dh = 64
    inv_freq = 1.0 / 10000.0 ** (np.arange(0, Dh, 2, dtype=np.float32) / Dh)
    fr = np.arange(16, dtype=np.float32)[:, None] * inv_freq[None, :]
    cosv = np.repeat(np.cos(fr).reshape(-1), 2)
    sv = np.sin(fr).reshape(-1)
    ssw = np.stack([-sv, sv], axis=-1).reshape(-1)
    cos_d = P.din("cos_bc", f32c(np.broadcast_to(cosv, (128, D))))
    ssw_d = P.din("ssw_bc", f32c(np.broadcast_to(ssw, (128, D))))

    mk = (lambda n, s: P.dout(n, s, BF16)) if taps else (lambda n, s: P.dtmp(n, s, BF16))
    xln = mk("xln", [S, D])
    q_s = mk("q_s", [S, D])
    k_s = mk("k_s", [S, D])
    v_s = mk("v_s", [S, D])
    ctxT = mk("ctxT", [D, S])

    P.push_scope()
    emit_ln(P, x_in, xln)
    P.pop_scope()
    if stop_after == "ln":
        return P

    P.push_scope()
    u = P.uid()
    ld, ldT, mm, dve, st = P.sem("ld"), P.sem("ldT"), P.sem("mm"), P.sem("dve"), P.sem("st")
    xlnT = [P.sbuf(f"xlnT{d}_{u}", [128, S], BF16) for d in range(8)]
    for d in range(8):
        P.op("sync", lambda e, d=d: e.dma_start_transpose(out=xlnT[d][:], in_=xln[:, d * 128:(d + 1) * 128]),
             ldT, 16)
    wsb = P.sbuf(f"wsb_{u}", [128, 8 * 3072], BF16)
    for d in range(8):
        P.op("sync", lambda e, d=d: e.dma_start(out=wsb[:, d * 3072:(d + 1) * 3072],
                                                in_=wqkv[d * 128:(d + 1) * 128, :]), ld, 16)
    bqkv = P.sbuf(f"bqkv_{u}", [128, 3072], F32)
    cosb = P.sbuf(f"cosb_{u}", [128, D], F32)
    sswb = P.sbuf(f"sswb_{u}", [128, D], F32)
    P.op("sync", lambda e: e.dma_start(out=bqkv[:], in_=bqkv_d), ld, 16)
    P.op("sync", lambda e: e.dma_start(out=cosb[:], in_=cos_d), ld, 16)
    P.op("sync", lambda e: e.dma_start(out=sswb[:], in_=ssw_d), ld, 16)
    qf = [P.sbuf(f"qf{i}_{u}", [128, D], F32) for i in range(2)]
    kf = [P.sbuf(f"kf{i}_{u}", [128, D], F32) for i in range(2)]
    qb = [P.sbuf(f"qb{i}_{u}", [128, D], BF16) for i in range(2)]
    kb = [P.sbuf(f"kb{i}_{u}", [128, D], BF16) for i in range(2)]
    vb = [P.sbuf(f"vb{i}_{u}", [128, D], BF16) for i in range(2)]
    t1 = P.sbuf(f"t1_{u}", [128, D], F32)
    t2 = P.sbuf(f"t2_{u}", [128, D], F32)
    ps = [P.psum(f"psq{i}_{u}", [128, 512], F32) for i in range(4)]
    P.wait("tensor", ldT, ldT.n)
    P.wait("tensor", ld, ld.n)
    P.wait("vector", ld, ld.n)
    stq = Stream(P, "stq")
    ps_guard = {}
    buf_guard = {}
    for i in range(NT):
        b = i % 2
        if b in buf_guard:
            stq.wait("vector", i - 2)
        for c in range(6):
            pj = c % 4
            if pj in ps_guard:
                P.wait("tensor", dve, ps_guard[pj])
            for d in range(8):
                P.op("tensor", lambda e, i=i, c=c, d=d, pj=pj:
                     e.matmul(ps[pj][:, :], xlnT[d][:, i * 128:(i + 1) * 128],
                              wsb[:, d * 3072 + c * 512:d * 3072 + (c + 1) * 512],
                              start=(d == 0), stop=(d == 7)), mm, 1)
            P.wait("vector", mm, mm.n)
            if c < 2:
                P.op("vector", lambda e, b=b, c=c, pj=pj:
                     e.tensor_add(out=qf[b][:, c * 512:(c + 1) * 512], in0=ps[pj][:, :],
                                  in1=bqkv[:, c * 512:(c + 1) * 512]), dve, 1)
            elif c < 4:
                P.op("vector", lambda e, b=b, c=c, pj=pj:
                     e.tensor_add(out=kf[b][:, (c - 2) * 512:(c - 1) * 512], in0=ps[pj][:, :],
                                  in1=bqkv[:, c * 512:(c + 1) * 512]), dve, 1)
            else:
                P.op("vector", lambda e, b=b, c=c, pj=pj:
                     e.tensor_add(out=vb[b][:, (c - 4) * 512:(c - 3) * 512], in0=ps[pj][:, :],
                                  in1=bqkv[:, c * 512:(c + 1) * 512]), dve, 1)
            ps_guard[pj] = dve.n
        plan_rope_interleaved(P, qf[b], qb[b], cosb, sswb, t1, t2)
        plan_rope_interleaved(P, kf[b], kb[b], cosb, sswb, t1, t2, mark=(dve, 1))
        P.wait("gpsimd", dve, dve.n)
        stq.dma("gpsimd", i, [
            lambda e, b=b, i=i: e.dma_start(out=q_s[i * 128:(i + 1) * 128, :], in_=qb[b][:]),
            lambda e, b=b, i=i: e.dma_start(out=k_s[i * 128:(i + 1) * 128, :], in_=kb[b][:]),
            lambda e, b=b, i=i: e.dma_start(out=v_s[i * 128:(i + 1) * 128, :], in_=vb[b][:]),
        ])
        buf_guard[b] = 1
    P.pop_scope()
    if stop_after == "qkv":
        return P

    P.push_scope()
    emit_attention(P, nheads=nheads, q_dram=q_s, k_dram=k_s, v_dram=v_s, ctxT_dram=ctxT,
                   causal=False, scale=1.0 / math.sqrt(64.0), stage=globals().get('_ATTN_STAGE', 99))
    P.pop_scope()
    if stop_after == "attn":
        return P

    P.push_scope()
    u = P.uid()
    ld, mm, dve, st = P.sem("ld"), P.sem("mm"), P.sem("dve"), P.sem("st")
    ctxTs = [P.sbuf(f"cT{d}_{u}", [128, S], BF16) for d in range(8)]
    for d in range(8):
        P.op("sync", lambda e, d=d: e.dma_start(out=ctxTs[d][:], in_=ctxT[d * 128:(d + 1) * 128, :]), ld, 16)
    wfcs = P.sbuf(f"wfcs_{u}", [128, 8 * D], BF16)
    for d in range(8):
        P.op("sync", lambda e, d=d: e.dma_start(out=wfcs[:, d * D:(d + 1) * D],
                                                in_=wfc[d * 128:(d + 1) * 128, :]), ld, 16)
    bfc = P.sbuf(f"bfc_{u}", [128, D], F32)
    P.op("sync", lambda e: e.dma_start(out=bfc[:], in_=bfc_d), ld, 16)
    xres = [P.sbuf(f"xr{i}_{u}", [128, D], F32) for i in range(2)]
    xo = [P.sbuf(f"xo{i}_{u}", [128, D], F32) for i in range(2)]
    ps = [P.psum(f"psw{i}_{u}", [128, 512], F32) for i in range(4)]
    P.wait("tensor", ld, ld.n)
    P.wait("vector", ld, ld.n)
    ldr = Stream(P, "ldr")
    sto = Stream(P, "sto")
    ps_guard = {}
    buf_guard = {}
    for i in range(NT):
        b = i % 2
        if b in buf_guard:
            P.wait("sync", dve, buf_guard[b])
        ldr.dma("sync", i, lambda e, b=b, i=i: e.dma_start(out=xres[b][:], in_=x_in[i * 128:(i + 1) * 128, :]))
        for c in range(2):
            pj = c % 4
            if pj in ps_guard:
                P.wait("tensor", dve, ps_guard[pj])
            for d in range(8):
                P.op("tensor", lambda e, i=i, c=c, d=d, pj=pj:
                     e.matmul(ps[pj][:, :], ctxTs[d][:, i * 128:(i + 1) * 128],
                              wfcs[:, d * D + c * 512:d * D + (c + 1) * 512],
                              start=(d == 0), stop=(d == 7)), mm, 1)
            P.wait("vector", mm, mm.n)
            ldr.wait("vector", i)
            if b in buf_guard:
                sto.wait("vector", i - 2)
            P.op("vector", lambda e, b=b, c=c, pj=pj:
                 e.tensor_add(out=xo[b][:, c * 512:(c + 1) * 512], in0=ps[pj][:, :],
                              in1=bfc[:, c * 512:(c + 1) * 512]))
            P.op("vector", lambda e, b=b, c=c:
                 e.tensor_add(out=xo[b][:, c * 512:(c + 1) * 512], in0=xo[b][:, c * 512:(c + 1) * 512],
                              in1=xres[b][:, c * 512:(c + 1) * 512]), dve, 1)
            ps_guard[pj] = dve.n
        P.wait("gpsimd", dve, dve.n)
        sto.dma("gpsimd", i, lambda e, b=b, i=i: e.dma_start(out=x_out[i * 128:(i + 1) * 128, :], in_=xo[b][:]))
        buf_guard[b] = dve.n
    P.pop_scope()
    return P


def emit_rms_pass(P, h_dram, w_bc_sb, eps, dst_bf_dram, taps=(None, None), dst32_dram=None):
    u = P.uid()
    dve = P.sem("dve")
    ldh = Stream(P, "rmsld")
    sto = Stream(P, "rmsst")
    hb = [P.sbuf(f"rh{i}_{u}", [128, D], F32) for i in range(2)]
    ob = [P.sbuf(f"ro{i}_{u}", [128, D], BF16) for i in range(2)]
    ob32 = [P.sbuf(f"ro32{i}_{u}", [128, D], F32) for i in range(2)] if dst32_dram is not None else None
    trash = P.sbuf(f"rtr_{u}", [128, D], F32)
    sc = P.sbuf(f"rsc_{u}", [128, 8], F32)
    gpm = P.sem("gp")
    P.op("gpsimd", lambda e: e.memset(sc[:], 0.0), gpm, 1)
    P.wait("vector", gpm, gpm.n)
    epsc = P.constcol(eps)
    P.cwait("scalar")
    guard = [0, 0]
    stored = [False, False]
    h_tap, sc_tap = taps
    for i in range(NT):
        b = i % 2
        if guard[b]:
            P.wait("sync", dve, guard[b])
        ldh.dma("sync", i, lambda e, b=b, i=i: e.dma_start(out=hb[b][:], in_=h_dram[i * 128:(i + 1) * 128, :]))
        ldh.wait("vector", i)
        if stored[b]:
            sto.wait("vector", i - 2)
        plan_rms_stats(P, hb[b], trash, sc, 0, epsc, sc[:, 6:7])
        P.op("vector", lambda e, b=b: e.tensor_scalar_mul(out=trash[:], in0=hb[b][:], scalar1=sc[:, 6:7]))
        if dst32_dram is not None:
            P.op("vector", lambda e, b=b: e.tensor_mul(out=ob32[b][:], in0=trash[:], in1=w_bc_sb[:]))
        P.op("vector", lambda e, b=b: e.tensor_mul(out=ob[b][:], in0=trash[:], in1=w_bc_sb[:]), dve, 1)
        guard[b] = dve.n
        P.wait("gpsimd", dve, dve.n)
        fns = [lambda e, b=b, i=i: e.dma_start(out=dst_bf_dram[i * 128:(i + 1) * 128, :], in_=ob[b][:])]
        if dst32_dram is not None:
            fns.append(lambda e, b=b, i=i: e.dma_start(out=dst32_dram[i * 128:(i + 1) * 128, :], in_=ob32[b][:]))
        if h_tap is not None:
            fns.append(lambda e, b=b, i=i: e.dma_start(out=h_tap[i * 128:(i + 1) * 128, :], in_=hb[b][:]))
        if sc_tap is not None:
            fns.append(lambda e, i=i: e.dma_start(out=sc_tap[i * 128:(i + 1) * 128, :], in_=sc[:]))
        sto.dma("gpsimd", i, fns)
        stored[b] = True


def build_l0_ffn(W, taps=False, stop_after=None):
    P = Prog("l0f")
    x_in = P.din("x", per_core=((S, D), F32))
    x_out = P.dout("x_out", [S, D], F32)
    wq, wk, wv, wo = (np.asarray(W[k], np.float32) for k in ("wq", "wk", "wv", "wo"))
    wqkvT_d = P.din("wqkvT", bf16(np.concatenate([wq.T, wk.T, wv.T], axis=1)))
    woT_d = P.din("woT", bf16(wo.T))
    anw_d = P.din("anw_bc", f32c(np.broadcast_to(np.asarray(W["attn_norm_w"], np.float32), (128, D))))
    mnw_d = P.din("mnw_bc", f32c(np.broadcast_to(np.asarray(W["moe_norm_w"], np.float32), (128, D))))
    gwT_d = P.din("gwT", bf16(np.asarray(W["gate_w"], np.float32).T))
    w1 = np.asarray(W["w1"], np.float32)
    w2 = np.asarray(W["w2"], np.float32)
    b1 = np.asarray(W["b1"], np.float32)
    b2 = np.asarray(W["b2"], np.float32)
    w1T_d = P.din("w1T", bf16(w1.transpose(0, 2, 1).reshape(8 * 1024, 4096)))
    w2T_d = P.din("w2T", bf16(w2.transpose(0, 2, 1).reshape(8 * 4096, 1024)))
    b1cm_d = P.din("b1cm", f32c(b1.reshape(8, 32, 128).transpose(0, 2, 1)))
    b2bc_d = P.din("b2bc", f32c(np.broadcast_to(b2[:, None, :], (8, 128, 1024))))
    HD = 64
    inv_freq = 1.0 / 10000.0 ** (np.arange(0, HD, 2, dtype=np.float32) / HD)
    fr = np.arange(S, dtype=np.float32)[:, None] * inv_freq
    emb = np.concatenate([fr, fr], axis=1)
    cose = np.cos(emb)
    sine = np.sin(emb)
    sins = np.concatenate([-sine[:, :32], sine[:, 32:]], axis=1)
    ropetab = np.concatenate([np.tile(cose, 16), np.tile(sins, 16)], axis=1)
    rope_d = P.din("ropetab", bf16(ropetab))
    ki = np.arange(128, dtype=np.float32)[:, None]
    qi = np.arange(512, dtype=np.float32)[None, :]
    masks = np.concatenate([(ki <= qi - 128.0 * j).astype(np.float32) for j in range(4)], axis=1)
    masks_d = P.din("cmasks", bf16(masks))

    mk = (lambda n, s, dt: P.dout(n, s, dt)) if taps else (lambda n, s, dt: P.dtmp(n, s, dt))
    t_s = mk("t_s", [S, D], BF16)
    xln32_s = mk("xln32_s", [S, D], F32)
    qkv_s = mk("qkv_s", [S, 3 * D], BF16)
    ctxT = mk("ctxT", [D, S], BF16)
    tn_s = mk("tn_s", [S, D], BF16)
    g_out = P.dout("g_out", [S, 8], F32) if taps else None
    h_tap = P.dout("h_tap", [S, D], F32) if taps else None
    sc_tap = P.dout("sc_tap", [S, 8], F32) if taps else None

    P.push_scope()
    anw_sb, _ = load_f32(P, anw_d, (128, D), "anw")
    P.wait("vector", P.sems["ldg"], P.sems["ldg"].n)
    emit_ln(P, x_in, None, xln32_dst=xln32_s, rms_w_bc=anw_sb, t_dst=t_s, rms_eps=1e-6)
    P.pop_scope()

    P.push_scope()
    srcT, _ = load_cm(P, t_s, "tT")
    emit_qkv_rope_stream(P, srcT, wqkvT_d, None, qkv_s, rope_d, "half")
    P.pop_scope()

    P.push_scope()
    masks_sb, _ = load_f32(P, masks_d, (128, 2048), "msk", dt=BF16)
    P.wait("vector", P.sems["ldg"], P.sems["ldg"].n)
    emit_attention(P, nheads=16, q_dram=qkv_s[:, 0:D], k_dram=qkv_s[:, D:2 * D],
                   v_dram=qkv_s[:, 2 * D:3 * D], ctxT_dram=ctxT, causal=True,
                   scale=1.0 / 8.0, masks_bc=masks_sb)
    P.pop_scope()

    h_s = P.dtmp("h_s", [S, D], F32)
    P.push_scope()
    u = P.uid()
    dve = P.sem("dve")
    ldr = Stream(P, "ldr")
    ldl = Stream(P, "ldl")
    sto = Stream(P, "stm")
    xres = [P.sbuf(f"xr{i}_{u}", [128, D], F32) for i in range(2)]
    xl32 = [P.sbuf(f"xl{i}_{u}", [128, D], F32) for i in range(2)]
    hb = [P.sbuf(f"hb{i}_{u}", [128, D], F32) for i in range(2)]
    xpo = [P.sbuf(f"xp{i}_{u}", [128, D], F32) for i in range(2)]
    pre_guard = [0, 0]
    stored = [False, False]

    def pre(i):
        b = i % 2
        if pre_guard[b]:
            P.wait("sync", dve, pre_guard[b])
        ldr.dma("sync", i, lambda e, b=b, i=i: e.dma_start(out=xres[b][:], in_=x_in[i * 128:(i + 1) * 128, :]))
        ldl.dma("sync", i, lambda e, b=b, i=i: e.dma_start(out=xl32[b][:],
                                                           in_=xln32_s[i * 128:(i + 1) * 128, :]))

    def evac(i, c0, ps):
        b = i % 2
        if c0 == 0:
            ldr.wait("vector", i)
            ldl.wait("vector", i)
            if stored[b]:
                sto.wait("vector", i - 2)
        P.op("vector", lambda e, b=b, c0=c0, ps=ps:
             e.tensor_add(out=hb[b][:, c0:c0 + 512], in0=ps[0:128, 0:512], in1=xl32[b][:, c0:c0 + 512]))
        P.op("vector", lambda e, b=b, c0=c0:
             e.tensor_add(out=xpo[b][:, c0:c0 + 512], in0=hb[b][:, c0:c0 + 512],
                          in1=xres[b][:, c0:c0 + 512]), dve, 1)
        if c0 + 512 >= D:
            pre_guard[b] = dve.n
            P.wait("gpsimd", dve, dve.n)
            sto.dma("gpsimd", i, [
                lambda e, b=b, i=i: e.dma_start(out=x_out[i * 128:(i + 1) * 128, :], in_=xpo[b][:]),
                lambda e, b=b, i=i: e.dma_start(out=h_s[i * 128:(i + 1) * 128, :], in_=hb[b][:]),
            ])
            stored[b] = True
        return dve.n

    emit_ctx_proj(P, ctxT, woT_d, D, pre, evac)
    P.pop_scope()

    P.push_scope()
    u = P.uid()
    mnw_sb, _ = load_f32(P, mnw_d, (128, D), "mnw")
    P.wait("vector", P.sems["ldg"], P.sems["ldg"].n)
    emit_rms_pass(P, h_s, mnw_sb, 1e-6, tn_s, taps=(h_tap, sc_tap))
    P.pop_scope()
    if stop_after == "mid":
        return P

    P.push_scope()
    u = P.uid()
    g_sb = P.sbuf(f"gsb_{u}", [128, NT * 8], F32)
    P.push_scope()
    mm, dve, act = P.sem("mm"), P.sem("dve"), P.sem("act")
    tnT, _ = load_cm(P, tn_s, "tnT")
    gw_sb, _ = load_w(P, gwT_d, 8, "gw")
    ps_g = [P.psum(f"psg{i}_{u}", [128, 8], F32) for i in range(2)]
    sc = P.sbuf(f"gsc_{u}", [128, 8], F32)
    eqt = P.sbuf(f"geq_{u}", [128, 8], F32)
    mskt = P.sbuf(f"gms_{u}", [128, 8], F32)
    ext = P.sbuf(f"gex_{u}", [128, 8], F32)
    ldg = P.sems["ldg"]
    P.wait("tensor", ldg, ldg.n)
    ps_guard = {}
    for i in range(NT):
        pj = i % 2
        if pj in ps_guard:
            P.wait("tensor", dve, ps_guard[pj])
        for d in range(8):
            P.op("tensor", lambda e, i=i, d=d, pj=pj:
                 e.matmul(ps_g[pj][:, :], tnT[d][:, i * 128:(i + 1) * 128],
                          gw_sb[:, d * 8:(d + 1) * 8], start=(d == 0), stop=(d == 7)), mm, 1)
        P.wait("vector", mm, mm.n)
        P.op("vector", lambda e, pj=pj: e.reduce_max(out=sc[:, 0:1], in_=ps_g[pj][:, :], axis=AX.X), dve, 1)
        P.wait("vector", dve, dve.n)
        P.op("vector", lambda e, pj=pj: e.tensor_single_scalar(out=eqt[:], in_=ps_g[pj][:, :],
                                                               scalar=sc[:, 0:1], op=ALU.is_equal), dve, 1)
        P.wait("vector", dve, dve.n)
        P.op("vector", lambda e, pj=pj: e.scalar_tensor_tensor(out=mskt[:], in0=eqt[:], scalar=-1e30,
                                                               in1=ps_g[pj][:, :], op0=ALU.mult, op1=ALU.add),
             dve, 1)
        P.wait("vector", dve, dve.n)
        P.op("vector", lambda e: e.reduce_max(out=sc[:, 1:2], in_=mskt[:], axis=AX.X), dve, 1)
        P.wait("vector", dve, dve.n)
        P.op("vector", lambda e: e.tensor_sub(out=sc[:, 2:3], in0=sc[:, 1:2], in1=sc[:, 0:1]), dve, 1)
        P.op("vector", lambda e: e.tensor_scalar_mul(out=sc[:, 3:4], in0=sc[:, 0:1], scalar1=-1.0), dve, 1)
        P.wait("scalar", dve, dve.n)
        P.op("scalar", lambda e: e.activation(out=sc[:, 4:5], in_=sc[:, 2:3], func=AF.Exp), act, 1)
        P.wait("scalar", act, act.n)
        P.op("scalar", lambda e, pj=pj: e.activation(out=ext[:], in_=ps_g[pj][:, :], func=AF.Exp,
                                                     bias=sc[:, 3:4]), act, 1)
        P.wait("vector", act, act.n)
        P.op("vector", lambda e: e.tensor_scalar_add(out=sc[:, 5:6], in0=sc[:, 4:5], scalar1=1.0), dve, 1)
        P.wait("vector", dve, dve.n)
        P.op("vector", lambda e: e.reciprocal(out=sc[:, 6:7], in_=sc[:, 5:6]), dve, 1)
        P.wait("vector", dve, dve.n)
        P.op("vector", lambda e, pj=pj: e.tensor_single_scalar(out=mskt[:], in_=ps_g[pj][:, :],
                                                               scalar=sc[:, 1:2], op=ALU.is_ge), dve, 1)
        ps_guard[pj] = dve.n
        P.wait("vector", dve, dve.n)
        P.op("vector", lambda e: e.tensor_scalar_mul(out=ext[:], in0=ext[:], scalar1=sc[:, 6:7]), dve, 1)
        P.wait("vector", dve, dve.n)
        P.op("vector", lambda e, i=i: e.tensor_mul(out=g_sb[:, i * 8:(i + 1) * 8], in0=ext[:], in1=mskt[:]),
             dve, 1)
    if taps:
        P.wait("gpsimd", dve, dve.n)
        for i in range(NT):
            P.op("gpsimd", lambda e, i=i: e.dma_start(out=g_out[i * 128:(i + 1) * 128, :],
                                                      in_=g_sb[:, i * 8:(i + 1) * 8]), P.sem("acc"), 16)
    P.pop_scope()

    P.push_scope()
    u = P.uid()
    mm = P.sem("mm")
    acc = P.sem("acc")
    w1sb = P.sbuf(f"w1s_{u}", [128, 8 * 4096], BF16)
    w2sb = P.sbuf(f"w2s_{u}", [128, 32 * 1024], BF16)
    b1sb = P.sbuf(f"b1s_{u}", [128, 32], F32)
    b2sb = P.sbuf(f"b2s_{u}", [128, 1024], F32)
    tnc = [P.sbuf(f"tnc{i}_{u}", [128, 8 * 512], BF16) for i in range(2)]
    hidT = P.sbuf(f"hid_{u}", [128, 32 * 512], BF16)
    stage = [P.sbuf(f"stg{i}_{u}", [128, 512], F32) for i in range(2)]
    ps_h = [P.psum(f"psh{i}_{u}", [128, 512], F32) for i in range(2)]
    ps_e = [P.psum(f"pse{i}_{u}", [128, 512], F32) for i in range(2)]
    ldw = P.sem("ldw")
    ldtn = Stream(P, "ldtn")
    for e_ in range(8):
        if e_ > 0:
            P.wait("sync", mm, mm.n)
        for d in range(8):
            P.op("sync", lambda e, e_=e_, d=d: e.dma_start(out=w1sb[:, d * 4096:(d + 1) * 4096],
                 in_=w1T_d[e_ * 1024 + d * 128:e_ * 1024 + (d + 1) * 128, :]), ldw, 16)
        for ht in range(32):
            P.op("sync", lambda e, e_=e_, ht=ht: e.dma_start(out=w2sb[:, ht * 1024:(ht + 1) * 1024],
                 in_=w2T_d[e_ * 4096 + ht * 128:e_ * 4096 + (ht + 1) * 128, :]), ldw, 16)
        P.op("sync", lambda e, e_=e_: e.dma_start(out=b1sb[:], in_=b1cm_d[e_]), ldw, 16)
        P.op("sync", lambda e, e_=e_: e.dma_start(out=b2sb[:], in_=b2bc_d[e_]), ldw, 16)
        P.wait("sync", ldw, ldw.n)
        P.wait("tensor", ldw, ldw.n)
        P.wait("scalar", ldw, ldw.n)
        P.wait("vector", ldw, ldw.n)
        acc_before = acc.n
        P.wait("gpsimd", acc, acc_before)
        hid_guard = {}
        eo_guard = {}
        st_guard = {}
        for tc in range(8):
            tb = tc % 2
            if e_ + tc > 0:
                P.wait("sync", mm, mm.n)
            ldtn.dma("sync", e_ * 8 + tc, [
                (lambda e, tc=tc, d=d, tb=tb: e.dma_start_transpose(
                    out=tnc[tb][:, d * 512:(d + 1) * 512],
                    in_=tn_s[tc * 512:(tc + 1) * 512, d * 128:(d + 1) * 128]))
                for d in range(8)])
            ldtn.wait("tensor", e_ * 8 + tc)
            eo_end_prev = eo_guard.get("eo_end", 0)
            for ht in range(32):
                pj = ht % 2
                if pj in hid_guard:
                    P.wait("tensor", act, hid_guard[pj])
                if ht == 0 and eo_end_prev:
                    P.wait("scalar", mm, eo_end_prev)
                for d in range(8):
                    P.op("tensor", lambda e, ht=ht, d=d, pj=pj, tb=tb:
                         e.matmul(ps_h[pj][:, :], w1sb[:, d * 4096 + ht * 128:d * 4096 + (ht + 1) * 128],
                                  tnc[tb][:, d * 512:(d + 1) * 512], start=(d == 0), stop=(d == 7)), mm, 1)
                P.wait("scalar", mm, mm.n)
                P.op("scalar", lambda e, ht=ht, pj=pj: e.activation(out=hidT[:, ht * 512:(ht + 1) * 512],
                     in_=ps_h[pj][:, :], func=AF.Gelu, bias=b1sb[:, ht:ht + 1]), act, 1)
                hid_guard[pj] = act.n
            hid_done = act.n
            P.wait("tensor", act, hid_done)
            for tsub in range(4):
                for dc in range(2):
                    pj = (tsub * 2 + dc) % 2
                    if pj in eo_guard:
                        P.wait("tensor", dve, eo_guard[pj])
                    for ht in range(32):
                        P.op("tensor", lambda e, ht=ht, tsub=tsub, dc=dc, pj=pj:
                             e.matmul(ps_e[pj][:, :],
                                      hidT[:, ht * 512 + tsub * 128:ht * 512 + (tsub + 1) * 128],
                                      w2sb[:, ht * 1024 + dc * 512:ht * 1024 + (dc + 1) * 512],
                                      start=(ht == 0), stop=(ht == 31)), mm, 1)
                    tile = tc * 4 + tsub
                    sb = (tsub * 2 + dc) % 2
                    P.wait("vector", mm, mm.n)
                    if sb in st_guard:
                        P.wait("vector", acc, st_guard[sb])
                    P.op("vector", lambda e, pj=pj, dc=dc, sb=sb:
                         e.tensor_add(out=stage[sb][:], in0=ps_e[pj][:, :],
                                      in1=b2sb[:, dc * 512:(dc + 1) * 512]), dve, 1)
                    P.wait("vector", dve, dve.n)
                    P.op("vector", lambda e, sb=sb, tile=tile, e_=e_:
                         e.tensor_scalar_mul(out=stage[sb][:], in0=stage[sb][:],
                                             scalar1=g_sb[:, tile * 8 + e_:tile * 8 + e_ + 1]), dve, 1)
                    eo_guard[pj] = dve.n
                    P.wait("gpsimd", dve, dve.n)
                    P.op("gpsimd", lambda e, tile=tile, dc=dc, sb=sb:
                         e.dma_start(out=x_out[tile * 128:(tile + 1) * 128, dc * 512:(dc + 1) * 512],
                                     in_=stage[sb][:], accum_op=ALU.add), acc, 16)
                    st_guard[sb] = acc.n
            eo_guard["eo_end"] = mm.n
    P.pop_scope()
    P.pop_scope()
    return P


def emit_theta_ropetab(P, theta16_d, dst_dram):
    u = P.uid()
    dve, act, gp, mm = P.sem("dve"), P.sem("act"), P.sem("gp"), P.sem("mm")
    th_row, _ = load_f32(P, theta16_d, (1, 512), "th")
    ones1 = P.sbuf(f"to1_{u}", [1, 128], F32)
    thbc = P.sbuf(f"thbc_{u}", [128, 512], F32)
    iot = P.sbuf(f"iot_{u}", [128, 1], F32)
    ioti = P.sbuf(f"ioti_{u}", [128, 1], I32)
    sv = P.sbuf(f"sv_{u}", [128, 1], F32)
    ang = P.sbuf(f"ang_{u}", [128, 512], F32)
    r1 = P.sbuf(f"r1_{u}", [128, 512], F32)
    r2 = P.sbuf(f"r2_{u}", [128, 512], F32)
    ob = [P.sbuf(f"tob{i}_{u}", [128, 1024], BF16) for i in range(2)]
    ps = P.psum(f"pst_{u}", [128, 512], F32)
    sto = Stream(P, "rts")
    P.op("gpsimd", lambda e: e.memset(ones1[:], 1.0))
    P.op("gpsimd", lambda e: e.iota(out=ioti[:], pattern=[[0, 1]], base=0, channel_multiplier=1), gp, 1)
    P.wait("vector", gp, gp.n)
    P.op("vector", lambda e: e.tensor_copy(out=iot[:], in_=ioti[:]), dve, 1)
    P.wait("tensor", gp, gp.n)
    P.wait("tensor", P.sems["ldg"], P.sems["ldg"].n)
    P.op("tensor", lambda e: e.matmul(ps[0:128, 0:512], ones1[0:1, 0:128], th_row[0:1, :],
                                      start=True, stop=True), mm, 1)
    P.wait("vector", mm, mm.n)
    P.op("vector", lambda e: e.tensor_copy(out=thbc[:], in_=ps[0:128, 0:512]), dve, 1)
    P.wait("vector", dve, dve.n)
    TWO_PI = 2.0 * math.pi
    MAGIC = 12582912.0
    stored = [False, False]
    for i in range(NT):
        b = i % 2
        P.op("vector", lambda e, i=i: e.tensor_scalar_add(out=sv[:], in0=iot[:], scalar1=float(128 * i)),
             dve, 1)
        P.wait("vector", dve, dve.n)
        if stored[b]:
            sto.wait("vector", i - 2)
        P.op("vector", lambda e: e.tensor_scalar_mul(out=ang[:], in0=thbc[:], scalar1=sv[:, 0:1]))
        P.op("vector", lambda e: e.tensor_scalar_mul(out=r1[:], in0=ang[:], scalar1=1.0 / TWO_PI))
        P.op("vector", lambda e: e.tensor_scalar_add(out=r1[:], in0=r1[:], scalar1=MAGIC))
        P.op("vector", lambda e: e.tensor_scalar_add(out=r1[:], in0=r1[:], scalar1=-MAGIC))
        P.op("vector", lambda e: e.scalar_tensor_tensor(out=r2[:], in0=r1[:], scalar=-TWO_PI, in1=ang[:],
                                                        op0=ALU.mult, op1=ALU.add))
        P.op("vector", lambda e: e.tensor_copy(out=sv[:], in_=sv[:]), dve, 1)
        P.wait("scalar", dve, dve.n)
        P.op("scalar", lambda e, b=b: e.activation(out=ob[b][:, 512:1024], in_=r2[:], func=AF.Sin), act, 1)
        P.op("vector", lambda e: e.tensor_scalar_add(out=ang[:], in0=ang[:], scalar1=math.pi / 2))
        P.op("vector", lambda e: e.tensor_scalar_mul(out=r1[:], in0=ang[:], scalar1=1.0 / TWO_PI))
        P.op("vector", lambda e: e.tensor_scalar_add(out=r1[:], in0=r1[:], scalar1=MAGIC))
        P.op("vector", lambda e: e.tensor_scalar_add(out=r1[:], in0=r1[:], scalar1=-MAGIC))
        P.op("vector", lambda e: e.scalar_tensor_tensor(out=r2[:], in0=r1[:], scalar=-TWO_PI, in1=ang[:],
                                                        op0=ALU.mult, op1=ALU.add))
        P.op("vector", lambda e: e.tensor_copy(out=sv[:], in_=sv[:]), dve, 1)
        P.wait("scalar", dve, dve.n)
        P.op("scalar", lambda e, b=b: e.activation(out=ob[b][:, 0:512], in_=r2[:], func=AF.Sin), act, 1)
        P.wait("gpsimd", act, act.n)
        sto.dma("gpsimd", i, lambda e, b=b, i=i: e.dma_start(out=dst_dram[i * 128:(i + 1) * 128, :],
                                                             in_=ob[b][:]))
        stored[b] = True


def build_l1_ffn(W, taps=False):
    P = Prog("l1f")
    x_in = P.din("x", per_core=((S, D), F32))
    x_out = P.dout("x_out", [S, D], F32)
    INTER = 1024
    ER = 31
    qw = np.asarray(W["q_w"], np.float32)
    kw = np.asarray(W["k_w"], np.float32)
    vw = np.asarray(W["v_w"], np.float32)
    wqkv_d = P.din("wqkv", bf16(np.concatenate([qw, kw, vw], axis=1)))
    bqkv = np.concatenate([np.asarray(W["q_b"], np.float32), np.asarray(W["k_b"], np.float32),
                           np.asarray(W["v_b"], np.float32)])
    bqkv_d = P.din("bqkv_bc", f32c(np.broadcast_to(bqkv, (128, 3 * D))))
    ow_d = P.din("ow", bf16(np.asarray(W["o_w"], np.float32)))
    ob_d = P.din("ob_bc", f32c(np.broadcast_to(np.asarray(W["o_b"], np.float32), (128, D))))
    g1_d = P.din("g1_bc", f32c(np.broadcast_to(np.asarray(W["gamma1"], np.float32), (128, D))))
    g2_d = P.din("g2_bc", f32c(np.broadcast_to(np.asarray(W["gamma2"], np.float32), (128, D))))
    shw1_d = P.din("shw1", bf16(np.asarray(W["sh_w1"], np.float32)))
    shb1cm_d = P.din("shb1cm", f32c(np.asarray(W["sh_b1"], np.float32).reshape(8, 128).T))
    shw2_d = P.din("shw2", bf16(np.asarray(W["sh_w2"], np.float32)))
    shb2_d = P.din("shb2_bc", f32c(np.broadcast_to(np.asarray(W["sh_b2"], np.float32), (128, D))))
    rw_d = P.din("rw", bf16(np.asarray(W["router_w"], np.float32)))
    rb_d = P.din("rb_bc", f32c(np.broadcast_to(np.asarray(W["router_b"], np.float32), (128, ER))))
    rew1_d = P.din("rew1", bf16(np.asarray(W["re_w1"], np.float32).reshape(ER * D, INTER)))
    rew2_d = P.din("rew2", bf16(np.asarray(W["re_w2"], np.float32).reshape(ER * INTER, D)))
    reb1_d = P.din("reb1cm", f32c(np.asarray(W["re_b1"], np.float32).reshape(ER, 8, 128).transpose(0, 2, 1)))
    reb2_d = P.din("reb2bc", f32c(np.broadcast_to(np.asarray(W["re_b2"], np.float32)[:, None, :],
                                                  (ER, 128, D))))
    theta16_d = P.din("theta16", f32c(np.tile(np.asarray(W["theta"], np.float32), 16).reshape(1, 512)))
    ki = np.arange(128, dtype=np.float32)[:, None]
    qi = np.arange(512, dtype=np.float32)[None, :]
    masks = np.concatenate([(ki <= qi - 128.0 * j).astype(np.float32) for j in range(4)], axis=1)
    masks_d = P.din("cmasks", bf16(masks))

    mk = (lambda n, s, dt: P.dout(n, s, dt)) if taps else (lambda n, s, dt: P.dtmp(n, s, dt))
    xn_s = mk("xn_s", [S, D], BF16)
    xn32_s = mk("xn32_s", [S, D], F32)
    ropetab_s = mk("ropetab_s", [S, 1024], BF16)
    qkv_s = mk("qkv_s", [S, 3 * D], BF16)
    ctxT = mk("ctxT", [D, S], BF16)
    ao_s = mk("ao_s", [S, D], F32)
    an_s = mk("an_s", [S, D], BF16)
    an32_s = mk("an32_s", [S, D], F32)
    shT_s = mk("shT_s", [INTER, S], BF16)

    P.push_scope()
    g1_sb, _ = load_f32(P, g1_d, (128, D), "g1")
    P.wait("vector", P.sems["ldg"], P.sems["ldg"].n)
    emit_ln(P, x_in, None, rms_w_bc=g1_sb, t_dst=xn_s, rms_eps=1e-5, t32_dst=xn32_s)
    P.pop_scope()

    P.push_scope()
    emit_theta_ropetab(P, theta16_d, ropetab_s)
    P.pop_scope()

    P.push_scope()
    srcT, _ = load_cm(P, xn_s, "xnT")
    emit_qkv_rope_stream(P, srcT, wqkv_d, bqkv_d, qkv_s, ropetab_s, "split")
    P.pop_scope()

    P.push_scope()
    masks_sb, _ = load_f32(P, masks_d, (128, 2048), "msk", dt=BF16)
    P.wait("vector", P.sems["ldg"], P.sems["ldg"].n)
    emit_attention(P, nheads=16, q_dram=qkv_s[:, 0:D], k_dram=qkv_s[:, D:2 * D],
                   v_dram=qkv_s[:, 2 * D:3 * D], ctxT_dram=ctxT, causal=True,
                   scale=1.0 / 8.0, masks_bc=masks_sb)
    P.pop_scope()

    P.push_scope()
    u = P.uid()
    dve = P.sem("dve")
    ob_sb, _ = load_f32(P, ob_d, (128, D), "obb")
    ldl = Stream(P, "ldl")
    sto = Stream(P, "stm")
    xl32 = [P.sbuf(f"xl{i}_{u}", [128, D], F32) for i in range(2)]
    hb = [P.sbuf(f"hb{i}_{u}", [128, D], F32) for i in range(2)]
    pre_guard = [0, 0]
    stored = [False, False]

    def pre(i):
        b = i % 2
        if pre_guard[b]:
            P.wait("sync", dve, pre_guard[b])
        ldl.dma("sync", i, lambda e, b=b, i=i: e.dma_start(out=xl32[b][:],
                                                           in_=xn32_s[i * 128:(i + 1) * 128, :]))

    def evac(i, c0, ps):
        b = i % 2
        if c0 == 0:
            ldl.wait("vector", i)
            if stored[b]:
                sto.wait("vector", i - 2)
        P.op("vector", lambda e, b=b, c0=c0, ps=ps:
             e.tensor_add(out=hb[b][:, c0:c0 + 512], in0=ps[0:128, 0:512], in1=ob_sb[:, c0:c0 + 512]))
        P.op("vector", lambda e, b=b, c0=c0:
             e.tensor_add(out=hb[b][:, c0:c0 + 512], in0=hb[b][:, c0:c0 + 512],
                          in1=xl32[b][:, c0:c0 + 512]), dve, 1)
        if c0 + 512 >= D:
            pre_guard[b] = dve.n
            P.wait("gpsimd", dve, dve.n)
            sto.dma("gpsimd", i, lambda e, b=b, i=i: e.dma_start(out=ao_s[i * 128:(i + 1) * 128, :],
                                                                 in_=hb[b][:]))
            stored[b] = True
        return dve.n

    emit_ctx_proj(P, ctxT, ow_d, D, pre, evac)
    P.pop_scope()

    P.push_scope()
    g2_sb, _ = load_f32(P, g2_d, (128, D), "g2")
    P.wait("vector", P.sems["ldg"], P.sems["ldg"].n)
    emit_rms_pass(P, ao_s, g2_sb, 1e-5, an_s, dst32_dram=an32_s)
    P.pop_scope()

    P.push_scope()
    anT, _ = load_cm(P, an_s, "anT")
    shb1_sb, _ = load_f32(P, shb1cm_d, (128, 8), "shb1")
    P.wait("scalar", P.sems["ldg"], P.sems["ldg"].n)
    emit_proj_cm(P, anT, shw1_d, shT_s, name="sh", gelu_bias_cm=shb1_sb)
    P.pop_scope()

    P.push_scope()
    u = P.uid()
    dve = P.sem("dve")
    shb2_sb, _ = load_f32(P, shb2_d, (128, D), "shb2")
    ldr = Stream(P, "ldr")
    lda = Stream(P, "lda")
    ldn = Stream(P, "ldn")
    sto = Stream(P, "stf")
    xres = [P.sbuf(f"fx{i}_{u}", [128, D], F32) for i in range(2)]
    aot = [P.sbuf(f"fa{i}_{u}", [128, D], F32) for i in range(2)]
    ant = [P.sbuf(f"fn{i}_{u}", [128, D], F32) for i in range(2)]
    xo = [P.sbuf(f"fo{i}_{u}", [128, D], F32) for i in range(2)]
    pre_guard = [0, 0]
    stored = [False, False]

    def pre2(i):
        b = i % 2
        if pre_guard[b]:
            P.wait("sync", dve, pre_guard[b])
        ldr.dma("sync", i, lambda e, b=b, i=i: e.dma_start(out=xres[b][:], in_=x_in[i * 128:(i + 1) * 128, :]))
        lda.dma("sync", i, lambda e, b=b, i=i: e.dma_start(out=aot[b][:], in_=ao_s[i * 128:(i + 1) * 128, :]))
        ldn.dma("sync", i, lambda e, b=b, i=i: e.dma_start(out=ant[b][:], in_=an32_s[i * 128:(i + 1) * 128, :]))

    def evac2(i, c0, ps):
        b = i % 2
        if c0 == 0:
            ldr.wait("vector", i)
            lda.wait("vector", i)
            ldn.wait("vector", i)
            if stored[b]:
                sto.wait("vector", i - 2)
        P.op("vector", lambda e, b=b, c0=c0, ps=ps:
             e.tensor_add(out=xo[b][:, c0:c0 + 512], in0=ps[0:128, 0:512], in1=shb2_sb[:, c0:c0 + 512]))
        P.op("vector", lambda e, b=b, c0=c0:
             e.tensor_add(out=xo[b][:, c0:c0 + 512], in0=xo[b][:, c0:c0 + 512], in1=xres[b][:, c0:c0 + 512]))
        P.op("vector", lambda e, b=b, c0=c0:
             e.tensor_add(out=xo[b][:, c0:c0 + 512], in0=xo[b][:, c0:c0 + 512], in1=aot[b][:, c0:c0 + 512]))
        P.op("vector", lambda e, b=b, c0=c0:
             e.tensor_add(out=xo[b][:, c0:c0 + 512], in0=xo[b][:, c0:c0 + 512], in1=ant[b][:, c0:c0 + 512]),
             dve, 1)
        if c0 + 512 >= D:
            pre_guard[b] = dve.n
            P.wait("gpsimd", dve, dve.n)
            sto.dma("gpsimd", i, lambda e, b=b, i=i: e.dma_start(out=x_out[i * 128:(i + 1) * 128, :],
                                                                 in_=xo[b][:]))
            stored[b] = True
        return dve.n

    emit_ctx_proj(P, shT_s, shw2_d, D, pre2, evac2)
    P.pop_scope()

    P.push_scope()
    u = P.uid()
    g_sb = P.sbuf(f"gsb_{u}", [128, NT * 32], F32)
    P.push_scope()
    mm, dve, act = P.sem("mm"), P.sem("dve"), P.sem("act")
    anT, _ = load_cm(P, an_s, "anT2")
    rw_sb, _ = load_w(P, rw_d, ER, "rws")
    rb_sb, _ = load_f32(P, rb_d, (128, ER), "rbs")
    ps_g = [P.psum(f"psg{i}_{u}", [128, ER], F32) for i in range(2)]
    lg = P.sbuf(f"lg_{u}", [128, ER], F32)
    aff = P.sbuf(f"aff_{u}", [128, ER], F32)
    mskt = P.sbuf(f"gms_{u}", [128, ER], F32)
    sc = P.sbuf(f"gsc_{u}", [128, 8], F32)
    ldg = P.sems["ldg"]
    P.wait("tensor", ldg, ldg.n)
    P.wait("vector", ldg, ldg.n)
    ps_guard = {}
    for i in range(NT):
        pj = i % 2
        if pj in ps_guard:
            P.wait("tensor", dve, ps_guard[pj])
        for d in range(8):
            P.op("tensor", lambda e, i=i, d=d, pj=pj:
                 e.matmul(ps_g[pj][:, :], anT[d][:, i * 128:(i + 1) * 128],
                          rw_sb[:, d * ER:(d + 1) * ER], start=(d == 0), stop=(d == 7)), mm, 1)
        P.wait("vector", mm, mm.n)
        P.op("vector", lambda e, pj=pj: e.tensor_add(out=lg[:], in0=ps_g[pj][:, :], in1=rb_sb[:]), dve, 1)
        ps_guard[pj] = dve.n
        P.wait("vector", dve, dve.n)
        P.op("vector", lambda e: e.reduce_max(out=sc[:, 0:1], in_=lg[:], axis=AX.X), dve, 1)
        P.wait("vector", dve, dve.n)
        P.op("vector", lambda e: e.tensor_scalar_mul(out=sc[:, 1:2], in0=sc[:, 0:1], scalar1=-1.0), dve, 1)
        P.wait("scalar", dve, dve.n)
        P.op("scalar", lambda e: e.activation(out=aff[:], in_=lg[:], func=AF.Exp, bias=sc[:, 1:2]), act, 1)
        P.wait("vector", act, act.n)
        P.op("vector", lambda e: e.reduce_sum(out=sc[:, 2:3], in_=aff[:], axis=AX.X), dve, 1)
        P.wait("vector", dve, dve.n)
        P.op("vector", lambda e: e.reciprocal(out=sc[:, 3:4], in_=sc[:, 2:3]), dve, 1)
        P.wait("vector", dve, dve.n)
        P.op("vector", lambda e: e.tensor_scalar_mul(out=aff[:], in0=aff[:], scalar1=sc[:, 3:4]), dve, 1)
        P.wait("vector", dve, dve.n)
        P.op("vector", lambda e: e.reduce_max(out=sc[:, 4:5], in_=aff[:], axis=AX.X), dve, 1)
        for r in range(2):
            P.wait("vector", dve, dve.n)
            P.op("vector", lambda e, r=r: e.tensor_single_scalar(out=mskt[:], in_=aff[:],
                                                                 scalar=sc[:, 4 + r:5 + r], op=ALU.is_equal),
                 dve, 1)
            P.wait("vector", dve, dve.n)
            src_t = aff if r == 0 else lg
            P.op("vector", lambda e, r=r: e.scalar_tensor_tensor(out=lg[:], in0=mskt[:], scalar=-1e30,
                 in1=(aff[:] if r == 0 else lg[:]), op0=ALU.mult, op1=ALU.add), dve, 1)
            P.wait("vector", dve, dve.n)
            P.op("vector", lambda e, r=r: e.reduce_max(out=sc[:, 5 + r:6 + r], in_=lg[:], axis=AX.X), dve, 1)
        P.wait("vector", dve, dve.n)
        P.op("vector", lambda e: e.tensor_single_scalar(out=mskt[:], in_=aff[:], scalar=sc[:, 6:7],
                                                        op=ALU.is_ge), dve, 1)
        P.wait("vector", dve, dve.n)
        P.op("vector", lambda e, i=i: e.tensor_mul(out=g_sb[:, i * 32:i * 32 + ER], in0=aff[:], in1=mskt[:]),
             dve, 1)
    P.pop_scope()

    P.push_scope()
    u = P.uid()
    mm, dve, act = P.sem("mm"), P.sem("dve"), P.sem("act")
    acc = P.sem("acc")
    anT, _ = load_cm(P, an_s, "anT3")
    w1sb = P.sbuf(f"w1s_{u}", [128, 8 * INTER], BF16)
    w2sb = P.sbuf(f"w2s_{u}", [128, 8 * D], BF16)
    b1sb = P.sbuf(f"b1s_{u}", [128, 8], F32)
    b2sb = P.sbuf(f"b2s_{u}", [128, D], F32)
    hidT = P.sbuf(f"hid_{u}", [128, 8 * 1024], BF16)
    stage = [P.sbuf(f"stg{i}_{u}", [128, 512], F32) for i in range(2)]
    ps_h = [P.psum(f"psh{i}_{u}", [128, 512], F32) for i in range(2)]
    ps_e = [P.psum(f"pse{i}_{u}", [128, 512], F32) for i in range(2)]
    ldw = P.sem("ldw")
    ldg = P.sems["ldg"]
    P.wait("tensor", ldg, ldg.n)
    for e_ in range(ER):
        if e_ > 0:
            P.wait("sync", mm, mm.n)
        for d in range(8):
            P.op("sync", lambda e, e_=e_, d=d: e.dma_start(out=w1sb[:, d * INTER:(d + 1) * INTER],
                 in_=rew1_d[e_ * D + d * 128:e_ * D + (d + 1) * 128, :]), ldw, 16)
        for ht in range(8):
            P.op("sync", lambda e, e_=e_, ht=ht: e.dma_start(out=w2sb[:, ht * D:(ht + 1) * D],
                 in_=rew2_d[e_ * INTER + ht * 128:e_ * INTER + (ht + 1) * 128, :]), ldw, 16)
        P.op("sync", lambda e, e_=e_: e.dma_start(out=b1sb[:], in_=reb1_d[e_]), ldw, 16)
        P.op("sync", lambda e, e_=e_: e.dma_start(out=b2sb[:], in_=reb2_d[e_]), ldw, 16)
        P.wait("sync", ldw, ldw.n)
        P.wait("tensor", ldw, ldw.n)
        P.wait("scalar", ldw, ldw.n)
        P.wait("vector", ldw, ldw.n)
        acc_before = acc.n
        P.wait("gpsimd", acc, acc_before)
        hid_guard = {}
        eo_guard = {}
        st_guard = {}
        eo_end_prev = 0
        for tc in range(4):
            for ht in range(8):
                for half in range(2):
                    pj = (ht * 2 + half) % 2
                    if pj in hid_guard:
                        P.wait("tensor", act, hid_guard[pj])
                    if ht == 0 and half == 0 and eo_end_prev:
                        P.wait("scalar", mm, eo_end_prev)
                    for d in range(8):
                        P.op("tensor", lambda e, tc=tc, ht=ht, half=half, d=d, pj=pj:
                             e.matmul(ps_h[pj][:, :],
                                      w1sb[:, d * INTER + ht * 128:d * INTER + (ht + 1) * 128],
                                      anT[d][:, tc * 1024 + half * 512:tc * 1024 + (half + 1) * 512],
                                      start=(d == 0), stop=(d == 7)), mm, 1)
                    P.wait("scalar", mm, mm.n)
                    P.op("scalar", lambda e, ht=ht, half=half, pj=pj:
                         e.activation(out=hidT[:, ht * 1024 + half * 512:ht * 1024 + (half + 1) * 512],
                                      in_=ps_h[pj][:, :], func=AF.Gelu, bias=b1sb[:, ht:ht + 1]), act, 1)
                    hid_guard[pj] = act.n
            hid_done = act.n
            P.wait("tensor", act, hid_done)
            for tsub in range(8):
                for dc in range(2):
                    pj = (tsub * 2 + dc) % 2
                    if pj in eo_guard:
                        P.wait("tensor", dve, eo_guard[pj])
                    for ht in range(8):
                        P.op("tensor", lambda e, ht=ht, tsub=tsub, dc=dc, pj=pj:
                             e.matmul(ps_e[pj][:, :],
                                      hidT[:, ht * 1024 + tsub * 128:ht * 1024 + (tsub + 1) * 128],
                                      w2sb[:, ht * D + dc * 512:ht * D + (dc + 1) * 512],
                                      start=(ht == 0), stop=(ht == 7)), mm, 1)
                    tile = tc * 8 + tsub
                    sb = pj
                    P.wait("vector", mm, mm.n)
                    if sb in st_guard:
                        P.wait("vector", acc, st_guard[sb])
                    P.op("vector", lambda e, pj=pj, dc=dc, sb=sb:
                         e.tensor_add(out=stage[sb][:], in0=ps_e[pj][:, :],
                                      in1=b2sb[:, dc * 512:(dc + 1) * 512]), dve, 1)
                    P.wait("vector", dve, dve.n)
                    P.op("vector", lambda e, sb=sb, tile=tile, e_=e_:
                         e.tensor_scalar_mul(out=stage[sb][:], in0=stage[sb][:],
                                             scalar1=g_sb[:, tile * 32 + e_:tile * 32 + e_ + 1]), dve, 1)
                    eo_guard[pj] = dve.n
                    P.wait("gpsimd", dve, dve.n)
                    P.op("gpsimd", lambda e, tile=tile, dc=dc, sb=sb:
                         e.dma_start(out=x_out[tile * 128:(tile + 1) * 128, dc * 512:(dc + 1) * 512],
                                     in_=stage[sb][:], accum_op=ALU.add), acc, 16)
                    st_guard[sb] = acc.n
            eo_end_prev = mm.n
    P.pop_scope()
    P.pop_scope()
    return P


def build_l0_attn_a(W):
    P = Prog("l0aa")
    x_in = P.din("x", per_core=((S, D), F32))
    x_out = P.dout("x_out", [S, D], F32)
    A = np.asarray(W["A"], np.float32)
    P_A = P.din("A", f32c(A))
    P_AT = P.din("AT", f32c(A.T))
    ident64 = P.din("id64", f32c(np.eye(64, dtype=np.float32)))
    Bm_d = P.din("Bm", bf16(np.asarray(W["Bm"], np.float32)))
    WimpT_d = P.din("WimpT", f32c(np.asarray(W["W_imp"], np.float32).T))
    WrT_d = P.din("WrT", bf16(np.asarray(W["W_router"], np.float32).T))
    comp_d = P.din("comp", f32c(np.asarray(W["compress"], np.float32).reshape(64, 1024 * 128)))
    KT_d = P.din("KT", bf16(np.asarray(W["K"], np.float32).T))
    V_d = P.din("V", f32c(np.asarray(W["V"], np.float32)))
    NK = 32768
    CW = 4096
    NCK = NK // CW

    xln = P.dtmp("xln", [S, D], BF16)
    scf = P.dtmp("scf", [1, 1024 * 128], BF16)
    nwt_d = P.dtmp("nwt", [64, 1], BF16)

    P.push_scope()
    emit_ln(P, x_in, xln)
    P.pop_scope()

    P.push_scope()
    qT = P.sbuf("qTsc", [128, S], BF16)

    P.push_scope()
    u = P.uid()
    mm, dve, act, gp = P.sem("mm"), P.sem("dve"), P.sem("act"), P.sem("gp")
    ldg = P.sem("ldg")
    xlnT, _ = load_cm(P, xln, "xT")
    bm_sb = P.sbuf(f"bm_{u}", [128, 8 * 64], BF16)
    a_sb, _ = load_f32(P, P_A, (64, 64), "a0")
    at_sb, _ = load_f32(P, P_AT, (64, 64), "at0")
    id_sb, _ = load_f32(P, ident64, (64, 64), "id")
    wimp_sb, _ = load_f32(P, WimpT_d, (64, 1024), "wimp")
    wrt_sb, _ = load_w(P, WrT_d, 64, "wrt")
    uT = P.sbuf(f"uT_{u}", [64, S], F32)
    lva = P.sbuf(f"lva_{u}", [64, 2048], F32)
    lvb = P.sbuf(f"lvb_{u}", [64, 2048], F32)
    ue = P.sbuf(f"ue_{u}", [64, 2048], F32)
    pw = P.sbuf(f"pw_{u}", [64, 64 * 14], F32)
    pwt = P.sbuf(f"pwt_{u}", [64, 64 * 14], F32)
    hp = P.sbuf(f"hp_{u}", [128, 8], F32)
    hpb = P.sbuf(f"hpb_{u}", [128, 8], BF16)
    lrow = P.sbuf(f"lrow_{u}", [1, S], F32)
    ltm = P.sbuf(f"ltm_{u}", [128, NT], F32)
    impb = P.sbuf(f"impb_{u}", [128, NT], BF16)
    tpb = P.sbuf(f"tpb_{u}", [128, 64], BF16)
    tptmp = P.sbuf(f"tpt_{u}", [128, 64], F32)
    sc = P.sbuf(f"sc_{u}", [128, 8], F32)
    glob = P.sbuf(f"glob_{u}", [1, 8], F32)
    gbc = P.sbuf(f"gbc_{u}", [128, 2], F32)
    nwrow = P.sbuf(f"nw_{u}", [1, 64], F32)
    nwb = P.sbuf(f"nwb_{u}", [1, 64], BF16)
    ones1 = P.sbuf(f"o1_{u}", [1, 128], F32)
    cmp_sb = [P.sbuf(f"cmp{i}_{u}", [64, 8192], BF16) for i in range(2)]
    scrow = P.sbuf(f"scr_{u}", [1, 8192], BF16)
    sccm = P.sbuf(f"sccm_{u}", [128, 8 * 128], BF16)
    ps_a = P.psum(f"psa_{u}", [128, 512], F32)
    ps_b = P.psum(f"psb_{u}", [128, 512], F32)
    ps_c = P.psum(f"psc_{u}", [64, 512], F32)
    ps_d = P.psum(f"psd_{u}", [128, 64], F32)
    ps_nw = P.psum(f"psnw_{u}", [1, 64], F32)
    P.op("gpsimd", lambda e: e.memset(ones1[:], 1.0), gp, 1)

    for d in range(8):
        P.op("sync", lambda e, d=d: e.dma_start(out=bm_sb[:, d * 64:(d + 1) * 64],
                                                in_=Bm_d[d * 128:(d + 1) * 128, :]), ldg, 16)
    P.wait("sync", ldg, ldg.n)
    P.wait("tensor", ldg, ldg.n)
    P.wait("tensor", gp, gp.n)
    P.wait("vector", ldg, ldg.n)
    P.wait("scalar", ldg, ldg.n)

    for qc in range(NCH):
        for d in range(8):
            P.op("tensor", lambda e, qc=qc, d=d: e.matmul(ps_c[:, :], bm_sb[:, d * 64:(d + 1) * 64],
                 xlnT[d][:, qc * 512:(qc + 1) * 512], start=(d == 0), stop=(d == 7)), mm, 1)
        P.wait("vector", mm, mm.n)
        P.op("vector", lambda e, qc=qc: e.tensor_copy(out=uT[:, qc * 512:(qc + 1) * 512], in_=ps_c[:, :]),
             dve, 1)
        P.wait("tensor", dve, dve.n)
    P.op("vector", lambda e: e.tensor_copy(out=pw[:, 0:64], in_=a_sb[:]))
    P.op("vector", lambda e: e.tensor_copy(out=pwt[:, 0:64], in_=at_sb[:]), dve, 1)
    P.wait("tensor", dve, dve.n)
    cur = uT
    curw = S
    for l in range(12):
        rows = curw // 2
        nxt = lva if (l % 2 == 0) else lvb
        P.op("vector", lambda e, cur=cur, rows=rows:
             e.tensor_copy(out=ue[:, 0:rows], in_=cur[:, 0:2 * rows].rearrange("p (r two) -> p r two", two=2)[:, :, 0:1]))
        P.op("vector", lambda e, cur=cur, rows=rows, nxt=nxt:
             e.tensor_copy(out=nxt[:, 0:rows], in_=cur[:, 0:2 * rows].rearrange("p (r two) -> p r two", two=2)[:, :, 1:2]),
             dve, 1)
        P.wait("tensor", dve, dve.n)
        for c0 in range(0, rows, 512):
            w = min(512, rows - c0)
            P.op("tensor", lambda e, l=l, c0=c0, w=w: e.matmul(ps_c[:, 0:w], pw[:, l * 64:(l + 1) * 64],
                 ue[:, c0:c0 + w], start=True, stop=True), mm, 1)
            P.wait("vector", mm, mm.n)
            P.op("vector", lambda e, c0=c0, w=w, nxt=nxt: e.tensor_add(out=nxt[:, c0:c0 + w],
                 in0=nxt[:, c0:c0 + w], in1=ps_c[:, 0:w]), dve, 1)
            P.wait("tensor", dve, dve.n)
        P.op("tensor", lambda e, l=l: e.matmul(ps_d[0:64, 0:64], pwt[:, l * 64:(l + 1) * 64],
             pw[:, l * 64:(l + 1) * 64], start=True, stop=True), mm, 1)
        P.wait("vector", mm, mm.n)
        P.op("vector", lambda e, l=l: e.tensor_copy(out=pw[:, (l + 1) * 64:(l + 2) * 64], in_=ps_d[0:64, 0:64]),
             dve, 1)
        P.wait("tensor", dve, dve.n)
        P.op("tensor", lambda e, l=l: e.transpose(ps_d[0:64, 0:64], pw[:, (l + 1) * 64:(l + 2) * 64],
                                                  id_sb[:]), mm, 1)
        P.wait("vector", mm, mm.n)
        P.op("vector", lambda e, l=l: e.tensor_copy(out=pwt[:, (l + 1) * 64:(l + 2) * 64],
                                                    in_=ps_d[0:64, 0:64]), dve, 1)
        P.wait("tensor", dve, dve.n)
        cur = nxt
        curw = rows
    for d in range(8):
        P.op("tensor", lambda e, d=d: e.matmul(ps_a[:, 0:1], wimp_sb[:, d * 128:(d + 1) * 128],
             cur[:, 0:1], start=True, stop=True), mm, 1)
        P.wait("vector", mm, mm.n)
        P.op("vector", lambda e, d=d: e.tensor_copy(out=hp[:, d:d + 1], in_=ps_a[:, 0:1]), dve, 1)
        P.wait("tensor", dve, dve.n)
    P.op("vector", lambda e: e.tensor_copy(out=hpb[:], in_=hp[:]), dve, 1)
    P.wait("tensor", dve, dve.n)
    for qc in range(NCH):
        for d in range(8):
            P.op("tensor", lambda e, qc=qc, d=d: e.matmul(ps_a[0:1, 0:512], hpb[:, d:d + 1],
                 xlnT[d][:, qc * 512:(qc + 1) * 512], start=(d == 0), stop=(d == 7)), mm, 1)
        P.wait("vector", mm, mm.n)
        P.op("vector", lambda e, qc=qc: e.tensor_copy(out=lrow[0:1, qc * 512:(qc + 1) * 512],
                                                      in_=ps_a[0:1, 0:512]), dve, 1)
        P.wait("tensor", dve, dve.n)
    for i in range(NT):
        for d in range(8):
            P.op("tensor", lambda e, i=i, d=d: e.matmul(ps_a[:, 0:1], xlnT[d][:, i * 128:(i + 1) * 128],
                 hpb[:, d:d + 1], start=(d == 0), stop=(d == 7)), mm, 1)
        P.wait("vector", mm, mm.n)
        P.op("vector", lambda e, i=i: e.tensor_copy(out=ltm[:, i:i + 1], in_=ps_a[:, 0:1]), dve, 1)
        P.wait("tensor", dve, dve.n)
    P.op("vector", lambda e: e.reduce_max(out=glob[0:1, 0:1], in_=lrow[:], axis=AX.X), dve, 1)
    P.wait("vector", dve, dve.n)
    P.op("vector", lambda e: e.tensor_scalar_mul(out=glob[0:1, 1:2], in0=glob[0:1, 0:1], scalar1=-1.0), dve, 1)
    P.wait("scalar", dve, dve.n)
    P.op("scalar", lambda e: e.activation(out=lrow[:], in_=lrow[:], func=AF.Exp, bias=glob[0:1, 1:2]), act, 1)
    P.wait("vector", act, act.n)
    P.op("vector", lambda e: e.reduce_sum(out=glob[0:1, 2:3], in_=lrow[:], axis=AX.X), dve, 1)
    P.wait("vector", dve, dve.n)
    P.op("vector", lambda e: e.reciprocal(out=glob[0:1, 3:4], in_=glob[0:1, 2:3]), dve, 1)
    P.wait("tensor", dve, dve.n)
    P.op("tensor", lambda e: e.matmul(ps_a[0:128, 0:1], ones1[0:1, 0:128], glob[0:1, 1:2],
                                      start=True, stop=True), mm, 1)
    P.op("tensor", lambda e: e.matmul(ps_a[0:128, 2:3], ones1[0:1, 0:128], glob[0:1, 3:4],
                                      start=True, stop=True), mm, 1)
    P.wait("vector", mm, mm.n)
    P.op("vector", lambda e: e.tensor_copy(out=gbc[:, 0:1], in_=ps_a[0:128, 0:1]))
    P.op("vector", lambda e: e.tensor_copy(out=gbc[:, 1:2], in_=ps_a[0:128, 2:3]), dve, 1)
    P.wait("scalar", dve, dve.n)
    P.op("scalar", lambda e: e.activation(out=ltm[:], in_=ltm[:], func=AF.Exp, bias=gbc[:, 0:1]), act, 1)
    P.wait("vector", act, act.n)
    P.op("vector", lambda e: e.tensor_scalar_mul(out=ltm[:], in0=ltm[:], scalar1=gbc[:, 1:2]), dve, 1)
    P.wait("vector", dve, dve.n)
    P.op("vector", lambda e: e.tensor_copy(out=impb[:], in_=ltm[:]), dve, 1)
    P.wait("tensor", dve, dve.n)
    for i in range(NT):
        for d in range(8):
            P.op("tensor", lambda e, i=i, d=d: e.matmul(ps_d[:, :], xlnT[d][:, i * 128:(i + 1) * 128],
                 wrt_sb[:, d * 64:(d + 1) * 64], start=(d == 0), stop=(d == 7)), mm, 1)
        P.wait("vector", mm, mm.n)
        P.op("vector", lambda e: e.reduce_max(out=sc[:, 0:1], in_=ps_d[:, :], axis=AX.X), dve, 1)
        P.wait("vector", dve, dve.n)
        P.op("vector", lambda e: e.tensor_scalar_mul(out=sc[:, 1:2], in0=sc[:, 0:1], scalar1=-1.0), dve, 1)
        P.wait("scalar", dve, dve.n)
        P.op("scalar", lambda e: e.activation(out=tptmp[:], in_=ps_d[:, :], func=AF.Exp, bias=sc[:, 1:2]),
             act, 1)
        P.wait("vector", act, act.n)
        P.op("vector", lambda e: e.reduce_sum(out=sc[:, 2:3], in_=tptmp[:], axis=AX.X), dve, 1)
        P.wait("vector", dve, dve.n)
        P.op("vector", lambda e: e.reciprocal(out=sc[:, 3:4], in_=sc[:, 2:3]), dve, 1)
        P.wait("vector", dve, dve.n)
        P.op("vector", lambda e: e.tensor_scalar_mul(out=tpb[:], in0=tptmp[:], scalar1=sc[:, 3:4]), dve, 1)
        P.wait("tensor", dve, dve.n)
        P.op("tensor", lambda e, i=i: e.matmul(ps_nw[0:1, :], impb[:, i:i + 1], tpb[:],
                                               start=(i == 0), stop=(i == NT - 1)), mm, 1)
        if i < NT - 1:
            P.wait("tensor", act, act.n)
    P.wait("vector", mm, mm.n)
    P.op("vector", lambda e: e.tensor_copy(out=nwrow[:], in_=ps_nw[0:1, :]), dve, 1)
    P.wait("vector", dve, dve.n)
    P.op("vector", lambda e: e.reduce_sum(out=glob[0:1, 4:5], in_=nwrow[:], axis=AX.X), dve, 1)
    P.wait("vector", dve, dve.n)
    P.op("vector", lambda e: e.tensor_scalar_add(out=glob[0:1, 5:6], in0=glob[0:1, 4:5], scalar1=1e-8), dve, 1)
    P.wait("vector", dve, dve.n)
    P.op("vector", lambda e: e.reciprocal(out=glob[0:1, 6:7], in_=glob[0:1, 5:6]), dve, 1)
    P.wait("vector", dve, dve.n)
    P.op("vector", lambda e: e.tensor_scalar_mul(out=nwb[:], in0=nwrow[:], scalar1=glob[0:1, 6:7]), dve, 1)
    st_nw = Stream(P, "stnw")
    P.wait("gpsimd", dve, dve.n)
    st_nw.dma("gpsimd", 0, lambda e: e.dma_start(out=nwt_d[0:64, 0:1].rearrange("a b -> b a") if False else nwt_d.rearrange("a b -> b a"), in_=nwb[:]))
    ldnw = Stream(P, "ldnw")
    st_nw.wait("sync", 0)
    nwt_sb = P.sbuf(f"nwts_{u}", [64, 1], BF16)
    ldnw.dma("sync", 0, lambda e: e.dma_start(out=nwt_sb[:], in_=nwt_d))
    ldcp = Stream(P, "ldcp")
    stsc = Stream(P, "stsc")
    ldnw.wait("tensor", 0)
    sc_done = [0, 0]
    for piece in range(16):
        pb = piece % 2
        if sc_done[pb]:
            P.wait("gpsimd", mm, sc_done[pb])
        ldcp.dma("gpsimd", piece, lambda e, piece=piece, pb=pb:
                 e.dma_start(out=cmp_sb[pb][:], in_=comp_d[:, piece * 8192:(piece + 1) * 8192]))
        ldcp.wait("tensor", piece)
        for j in range(16):
            P.op("tensor", lambda e, pb=pb, j=j: e.matmul(ps_a[0:1, 0:512], nwt_sb[:],
                 cmp_sb[pb][:, j * 512:(j + 1) * 512], start=True, stop=True), mm, 1)
            P.wait("vector", mm, mm.n)
            P.op("vector", lambda e, j=j: e.tensor_copy(out=scrow[0:1, j * 512:(j + 1) * 512],
                                                        in_=ps_a[0:1, 0:512]), dve, 1)
            P.wait("tensor", dve, dve.n)
        sc_done[pb] = mm.n
        P.wait("gpsimd", dve, dve.n)
        stsc.dma("gpsimd", piece, lambda e, piece=piece:
                 e.dma_start(out=scf[0:1, piece * 8192:(piece + 1) * 8192], in_=scrow[:]))
    ldsc = Stream(P, "ldsc")
    stsc.wait("sync", 14)
    stsc.wait("sync", 15)
    scv = scf.rearrange("one (d r) -> (one d) r", r=128)
    ldsc.dma("sync", 0, [(lambda e, d=d: e.dma_start(out=sccm[:, d * 128:(d + 1) * 128],
                                                     in_=scv[d * 128:(d + 1) * 128, :])) for d in range(8)])
    ldsc.wait("tensor", 0)
    for qc in range(NCH):
        for d in range(8):
            P.op("tensor", lambda e, qc=qc, d=d: e.matmul(ps_b[:, :], sccm[:, d * 128:(d + 1) * 128],
                 xlnT[d][:, qc * 512:(qc + 1) * 512], start=(d == 0), stop=(d == 7)), mm, 1)
        P.wait("vector", mm, mm.n)
        P.op("vector", lambda e, qc=qc: e.tensor_copy(out=qT[:, qc * 512:(qc + 1) * 512], in_=ps_b[:, :]),
             dve, 1)
        P.wait("tensor", dve, dve.n)
    P.pop_scope()

    P.push_scope()
    u = P.uid()
    mm, dve, act, gp = P.sem("mm"), P.sem("dve"), P.sem("act"), P.sem("gp")
    ldg = P.sem("ldg")
    ktsb = [P.sbuf(f"kt{j}_{u}", [128, NK // 4], BF16) for j in range(4)]
    for j in range(4):
        P.op("sync", lambda e, j=j: e.dma_start(out=ktsb[j][:], in_=KT_d[:, j * 8192:(j + 1) * 8192]), ldg, 16)
    P.wait("sync", ldg, ldg.n)
    iob = P.sbuf(f"iob_{u}", [128, CW], I32)
    consts_i = P.sbuf(f"ci_{u}", [128, 12], I32)
    scores = P.sbuf(f"scor_{u}", [128, CW], F32)
    enc = P.sbuf(f"enc_{u}", [128, CW], I32)
    cand = P.sbuf(f"cand_{u}", [128, 64], I32)
    g8 = P.sbuf(f"g8_{u}", [128, 8], I32)
    sep = P.sbuf(f"sep_{u}", [128, 8], F32)
    idx = P.sbuf(f"idx_{u}", [128, 8], I32)
    vals = P.sbuf(f"vals_{u}", [128, 8], F32)
    w8 = P.sbuf(f"w8_{u}", [128, 8], F32)
    sc2 = P.sbuf(f"sc2_{u}", [128, 4], F32)
    vsel = [P.sbuf(f"vsel{i}_{u}", [128, 8 * 1024], BF16) for i in range(2)]
    accb = [P.sbuf(f"acc{i}_{u}", [128, D], F32) for i in range(2)]
    xres = [P.sbuf(f"xr{i}_{u}", [128, D], F32) for i in range(2)]
    ps_s = [P.psum(f"pss{i}_{u}", [128, 512], F32) for i in range(2)]
    P.op("gpsimd", lambda e: e.iota(out=iob[:], pattern=[[1, CW]], base=0, channel_multiplier=0))
    P.op("gpsimd", lambda e: e.memset(consts_i[:, 0:1], -32768))
    P.op("gpsimd", lambda e: e.memset(consts_i[:, 1:2], 0x7FFF))
    for c in range(NCK):
        P.op("gpsimd", lambda e, c=c: e.memset(consts_i[:, 2 + c:3 + c], c << 12), gp, 1)
    P.wait("vector", gp, gp.n)
    P.wait("scalar", gp, gp.n)
    scale_c = P.constcol(1.0 / math.sqrt(128.0))
    P.cwait("scalar")
    ldr = Stream(P, "ldr")
    sto = Stream(P, "sto")
    gat = Stream(P, "gat")
    gat_free = [0, 0]
    res_guard = [0, 0]
    stored = [False, False]
    for i in range(NT):
        b = i % 2
        if res_guard[b]:
            P.wait("sync", dve, res_guard[b])
        ldr.dma("sync", i, lambda e, b=b, i=i: e.dma_start(out=xres[b][:], in_=x_in[i * 128:(i + 1) * 128, :]))
        enc_done = 0
        for c in range(NCK):
            for j8 in range(CW // 512):
                pj = j8 % 2
                ksb = ktsb[(c * CW + j8 * 512) // 8192]
                koff = (c * CW + j8 * 512) % 8192
                if i + c + j8 > 0:
                    P.wait("tensor", act, act.n)
                P.op("tensor", lambda e, i=i, pj=pj, ksb=ksb, koff=koff:
                     e.matmul(ps_s[pj][:, :], qT[:, i * 128:(i + 1) * 128], ksb[:, koff:koff + 512],
                              start=True, stop=True), mm, 1)
                P.wait("scalar", mm, mm.n)
                if j8 == 0 and enc_done:
                    P.wait("scalar", dve, enc_done)
                P.op("scalar", lambda e, pj=pj, j8=j8: e.copy(out=scores[:, j8 * 512:(j8 + 1) * 512],
                                                              in_=ps_s[pj][:, :]), act, 1)
            P.wait("vector", act, act.n)
            P.op("vector", lambda e: e.scalar_tensor_tensor(out=enc[:], in0=scores[:].bitcast(I32),
                 scalar=consts_i[:, 0:1], in1=iob[:], op0=ALU.bitwise_and, op1=ALU.bitwise_or))
            P.op("vector", lambda e: e.tensor_copy(out=sep[:], in_=sep[:]), dve, 1)
            enc_done = dve.n
            P.op("vector", lambda e, c=c: e.max(out=cand[:, c * 8:(c + 1) * 8].bitcast(F32),
                                                in_=enc[:].bitcast(F32)))
            P.op("vector", lambda e, c=c: e.tensor_single_scalar(out=cand[:, c * 8:(c + 1) * 8],
                 in_=cand[:, c * 8:(c + 1) * 8], scalar=consts_i[:, 2 + c:3 + c], op=ALU.bitwise_or), dve, 1)
        P.op("vector", lambda e: e.tensor_copy(out=sep[:], in_=sep[:]))
        P.op("vector", lambda e: e.max(out=g8[:].bitcast(F32), in_=cand[:].bitcast(F32)))
        P.op("vector", lambda e: e.tensor_copy(out=sep[:], in_=sep[:]), dve, 1)
        P.wait("vector", dve, dve.n)
        P.op("vector", lambda e: e.tensor_single_scalar(out=idx[:], in_=g8[:], scalar=consts_i[:, 1:2],
                                                        op=ALU.bitwise_and), dve, 1)
        P.op("vector", lambda e: e.tensor_single_scalar(out=vals[:].bitcast(I32), in_=g8[:],
                                                        scalar=consts_i[:, 0:1], op=ALU.bitwise_and), dve, 1)
        P.wait("vector", dve, dve.n)
        P.op("vector", lambda e: e.tensor_scalar_mul(out=sc2[:, 0:1], in0=vals[:, 0:1],
                                                     scalar1=-1.0 / math.sqrt(128.0)), dve, 1)
        P.wait("scalar", dve, dve.n)
        P.op("scalar", lambda e, sc_=scale_c: e.activation(out=w8[:], in_=vals[:], func=AF.Exp,
                                                           scale=sc_, bias=sc2[:, 0:1]), act, 1)
        P.wait("vector", act, act.n)
        P.op("vector", lambda e: e.reduce_sum(out=sc2[:, 1:2], in_=w8[:], axis=AX.X), dve, 1)
        P.wait("vector", dve, dve.n)
        P.op("vector", lambda e: e.reciprocal(out=sc2[:, 2:3], in_=sc2[:, 1:2]), dve, 1)
        P.wait("vector", dve, dve.n)
        P.op("vector", lambda e: e.tensor_scalar_mul(out=w8[:], in0=w8[:], scalar1=sc2[:, 2:3]), dve, 1)
        idx_done = dve.n
        P.wait("gpsimd", dve, idx_done)
        if gat_free[b]:
            P.wait("gpsimd", dve, gat_free[b])
        gat.dma("gpsimd", i, [(lambda e, b=b, k=k: e.indirect_dma_start(
            out=vsel[b][:, k * 1024:(k + 1) * 1024], out_offset=None, in_=V_d,
            in_offset=IndirectOffsetOnAxis(ap=idx[:, k:k + 1].bitcast(U32), axis=0)))
            for k in range(8)])
        gat.wait("vector", i)
        P.op("vector", lambda e, b=b: e.tensor_scalar_mul(out=accb[b][:], in0=vsel[b][:, 0:1024],
                                                          scalar1=w8[:, 0:1]))
        for k in range(1, 8):
            P.op("vector", lambda e, b=b, k=k: e.scalar_tensor_tensor(out=accb[b][:],
                 in0=vsel[b][:, k * 1024:(k + 1) * 1024], scalar=w8[:, k:k + 1], in1=accb[b][:],
                 op0=ALU.mult, op1=ALU.add))
        gat_free[b] = None
        P.op("vector", lambda e: e.tensor_copy(out=sep[:], in_=sep[:]), dve, 1)
        gat_free[b] = dve.n
        ldr.wait("vector", i)
        if stored[b]:
            sto.wait("vector", i - 2)
        P.op("vector", lambda e, b=b: e.tensor_add(out=accb[b][:], in0=accb[b][:], in1=xres[b][:]), dve, 1)
        res_guard[b] = dve.n
        P.wait("gpsimd", dve, dve.n)
        sto.dma("gpsimd", i, lambda e, b=b, i=i: e.dma_start(out=x_out[i * 128:(i + 1) * 128, :],
                                                             in_=accb[b][:]))
        stored[b] = True
    P.pop_scope()
    P.pop_scope()
    return P


_CACHE = {}


def kernel(**inputs):
    def get(name, builder, W):
        if name not in _CACHE:
            p = builder(W)
            p.finalize()
            _CACHE[name] = p
        return _CACHE[name]

    W_l0aa = {"A": inputs["L0_attn_a_A"], "Bm": inputs["L0_attn_a_Bm"],
              "W_imp": inputs["L0_attn_a_W_imp"], "W_router": inputs["L0_attn_a_W_router"],
              "compress": inputs["L0_attn_a_compress"], "K": inputs["L0_attn_a_K"],
              "V": inputs["L0_attn_a_V"]}
    W_l0f = {k: inputs["L0_ffn_" + k] for k in ["attn_norm_w", "wq", "wk", "wv", "wo",
                                                "moe_norm_w", "gate_w", "w1", "b1", "w2", "b2"]}
    W_l0ab = {"w_qkv": inputs["L0_attn_b_w_qkv"], "b_qkv": inputs["L0_attn_b_b_qkv"],
              "w_fc": inputs["L0_attn_b_w_fc"], "b_fc": inputs["L0_attn_b_b_fc"]}
    W_l1aa = {"Wq": inputs["L1_attn_a_Wq"], "Wk": inputs["L1_attn_a_Wk"],
              "Wv": inputs["L1_attn_a_Wv"], "Wo": inputs["L1_attn_a_Wo"],
              "decay_logit": inputs["L1_attn_a_decay_logit"],
              "out_scale": inputs["L1_attn_a_out_scale"]}
    W_l1f = {k: inputs["L1_ffn_" + k] for k in ["theta", "gamma1", "gamma2", "q_w", "q_b",
                                                "k_w", "k_b", "v_w", "v_b", "o_w", "o_b",
                                                "router_w", "router_b", "sh_w1", "sh_b1",
                                                "sh_w2", "sh_b2", "re_w1", "re_b1", "re_w2",
                                                "re_b2"]}
    W_l1ab = {"W_qkv": inputs["L1_attn_b_W_qkv"], "b_qkv": inputs["L1_attn_b_b_qkv"],
              "W_out": inputs["L1_attn_b_W_out"], "b_out": inputs["L1_attn_b_b_out"]}

    chain = [
        get("l0aa", build_l0_attn_a, W_l0aa),
        get("l0f", build_l0_ffn, W_l0f),
        get("l0ab", build_l0_attn_b, W_l0ab),
        _CACHE["l0f"],
        get("l1aa", build_l1_attn_a, W_l1aa),
        get("l1f", build_l1_ffn, W_l1f),
        get("l1ab", build_l1_attn_b, W_l1ab),
        _CACHE["l1f"],
    ]
    x = np.ascontiguousarray(inputs["L0_attn_a_x"].astype(np.float32))
    for prog in chain:
        res = prog.run(core_arrays={"x": [x[b] for b in range(8)]})
        x = np.stack([np.asarray(res.results[b]["x_out"]) for b in range(8)]).astype(np.float32)
    return x
